# version 22
# baseline (speedup 1.0000x reference)
import math
from contextlib import ExitStack

import numpy as np
import concourse.bass as bass
import concourse.mybir as mybir
from concourse.bass_utils import run_bass_kernel_spmd
from concourse.bass_core_rust import IndirectOffsetOnAxis

F32 = mybir.dt.float32
BF16 = mybir.dt.bfloat16
U32 = mybir.dt.uint32
AF = mybir.ActivationFunctionType
ALU = mybir.AluOpType

B, S, D, NH, HD, DF = 2, 4096, 2048, 16, 128, 8192
N_ITER = 56
EPS = 1e-5
SWAP32 = [i ^ 1 for i in range(32)]
PCLIP = 3.1415925


def _recalc():
    global NHL, TS, CH, NCH, NKT, NTT, KSEL, SCALE, DT_, FSL, DSL, GPH, QKW
    NHL = NH // 4
    TS = S // 4
    CH = 512 if S >= 2048 else max(HD, S // 8)
    NCH = S // CH
    NKT = S // HD
    NTT = TS // HD
    KSEL = S // 2
    SCALE = 1.0 / math.sqrt(HD)
    DT_ = D // HD
    FSL = DF // HD
    DSL = D // HD
    GPH = CH // HD
    QKW = NHL * HD


_recalc()


class Sem:
    def __init__(self, nc, es, name):
        self.h = es.enter_context(nc.semaphore(name))
        self.n = 0

    def inc(self, instr, k=1):
        instr.then_inc(self.h, k)
        self.n += k
        return instr


def build_nc(stop_after=None, taps=()):
    nc = bass.Bass("TRN2", target_bir_lowering=False, debug=False)

    x_row = nc.dram_tensor("x_row", [S, D], F32, kind="ExternalInput")
    x_sl = nc.dram_tensor("x_sl", [TS, D], F32, kind="ExternalInput")
    wq_g = nc.dram_tensor("wq_g", [D, NHL * HD], F32, kind="ExternalInput")
    wk_g = nc.dram_tensor("wk_g", [D, NHL * HD], F32, kind="ExternalInput")
    wv_g = nc.dram_tensor("wv_g", [D, NHL * HD], F32, kind="ExternalInput")
    wo_in = nc.dram_tensor("wo_in", [NH * HD, D], F32, kind="ExternalInput")
    wr_in = nc.dram_tensor("wr_in", [1, D], F32, kind="ExternalInput")
    br_in = nc.dram_tensor("br_in", [1, 1], F32, kind="ExternalInput")
    w1_in = nc.dram_tensor("w1_in", [DF, D], F32, kind="ExternalInput")
    b1c_in = nc.dram_tensor("b1c_in", [HD, DF // HD], F32, kind="ExternalInput")
    w2_in = nc.dram_tensor("w2_in", [D, DF], F32, kind="ExternalInput")
    b2c_in = nc.dram_tensor("b2c_in", [HD, D // HD], F32, kind="ExternalInput")
    ash_in = nc.dram_tensor("ash_in", [HD, S], F32, kind="ExternalInput")
    asl_in = nc.dram_tensor("asl_in", [HD, S], F32, kind="ExternalInput")
    ach_in = nc.dram_tensor("ach_in", [HD, S], F32, kind="ExternalInput")
    acl_in = nc.dram_tensor("acl_in", [HD, S], F32, kind="ExternalInput")
    tri_in = nc.dram_tensor("tri_in", [HD, HD], BF16, kind="ExternalInput")
    id_in = nc.dram_tensor("id_in", [HD, HD], F32, kind="ExternalInput")
    ones_in = nc.dram_tensor("ones_in", [HD, 1], F32, kind="ExternalInput")
    onesr_in = nc.dram_tensor("onesr_in", [1, HD], F32, kind="ExternalInput")
    agidx_in = nc.dram_tensor("agidx_in", [HD, NH], U32, kind="ExternalInput")
    scidx_in = nc.dram_tensor("scidx_in", [HD, 1], U32, kind="ExternalInput")

    out_sl = nc.dram_tensor("out_sl", [TS, D], F32, kind="ExternalOutput")
    tap_t = {}
    for name, shape, dt in (
        ("xln", [S, D], BF16), ("qT", [HD, NHL * S], BF16),
        ("kT", [HD, NHL * S], BF16), ("v", [HD, NKT * QKW], BF16),
        ("attnT", [NHL * HD, S], BF16), ("x2", [TS, D], F32),
        ("x2ln", [TS, D], BF16), ("sc", [1, TS], F32), ("thr", [1, 4], F32),
        ("atm", [HD, NH * TS], BF16), ("scg", [HD, S // HD], F32),
        ("hT", [HD, min(16, FSL) * TS], BF16), ("o2", [TS, D], F32),
    ):
        if name in taps:
            tap_t[name] = nc.dram_tensor("tap_" + name, shape, dt, kind="ExternalOutput")

    xln_hbm = nc.dram_tensor("xln_hbm", [S, D], BF16, kind="Internal")
    q_hbm = nc.dram_tensor("q_hbm", [HD, NHL * S], BF16, kind="Internal")
    x2_hbm = nc.dram_tensor("x2_hbm", [TS, D], F32, kind="Internal")
    x2ln_hbm = nc.dram_tensor("x2ln_hbm", [TS, D], BF16, kind="Internal")
    o2_hbm = nc.dram_tensor("o2_hbm", [D, TS], BF16, kind="Internal")
    HHL0 = max(1, (NH // 4) // 2)
    agin1a = nc.dram_tensor("agin1a", [HHL0 * HD, S], BF16, kind="Internal")
    agin1b = nc.dram_tensor("agin1b", [((NH // 4) - HHL0) * HD, S], BF16, kind="Internal")
    HHL = HHL0
    agout1a = nc.dram_tensor("agout1a", [8 * HHL * HD * (S // TS), TS], BF16, kind="Internal")
    agout1b = nc.dram_tensor("agout1b", [8 * (NHL - HHL) * HD * (S // TS), TS], BF16, kind="Internal")
    agin2 = nc.dram_tensor("agin2", [1, TS], F32, kind="Internal")
    agout2 = nc.dram_tensor("agout2", [8 * TS * HD // S, S // HD], F32, kind="Internal")

    es = ExitStack()

    def SB(stack, name, shape, dtype):
        return stack.enter_context(nc.sbuf_tensor(name, shape, dtype))

    def PS(stack, name, shape, dtype=F32):
        return stack.enter_context(nc.psum_tensor(name, shape, dtype))

    ones_row = SB(es, "ones_row", [1, HD], F32)
    ones_colb = SB(es, "ones_colb", [HD, 1], BF16)
    ones_colf = SB(es, "ones_colf", [HD, 1], F32)
    tri_sb = SB(es, "tri_sb", [HD, HD], BF16)
    id_sb = SB(es, "id_sb", [HD, HD], F32)
    wr_sb = SB(es, "wr_sb", [1, D], F32)
    br_sb = SB(es, "br_sb", [1, 1], F32)
    b1_sb = SB(es, "b1_sb", [HD, DF // HD], F32)
    b2c_sb = SB(es, "b2c_sb", [HD, D // HD], F32)
    agidx = SB(es, "agidx", [HD, NH], U32)
    scidx = SB(es, "scidx", [HD, 1], U32)
    sc_mine = SB(es, "sc_mine", [HD, NTT], F32)
    mask_sb = SB(es, "mask_sb", [HD, NTT], F32)
    small = SB(es, "small", [HD, 32], F32)
    tiny = SB(es, "tiny", [1, 32], F32)
    tiny_i = SB(es, "tiny_i", [1, 4], mybir.dt.int32)
    junkb = SB(es, "junkb", [HD, D], BF16)
    junkb2 = SB(es, "junkb2", [HD, D], BF16)

    sems = []

    def mk(name):
        s = Sem(nc, es, name)
        sems.append(s)
        return s

    s_cst = mk("s_cst"); s_x = mk("s_x"); s_st = mk("s_st"); s_sq = mk("s_sq")
    s_ln = mk("s_ln"); s_xw = mk("s_xw"); s_xt = mk("s_xt"); s_ang = mk("s_ang")
    s_xtp = [mk("s_xtp0"), mk("s_xtp1")]
    s_tab = mk("s_tab"); s_qk = mk("s_qk"); s_rope = mk("s_rope"); s_vev = mk("s_vev")
    s_vf = mk("s_vf"); s_sf = mk("s_sf")
    s_lg = mk("s_lg"); s_ex = mk("s_ex"); s_mk = mk("s_mk"); s_pv = mk("s_pv")
    s_rc = mk("s_rc"); s_rb = mk("s_rb"); s_ev = mk("s_ev"); s_ed = mk("s_ed")
    s_ag1 = mk("s_ag1"); s_am = mk("s_am"); s_wo = mk("s_wo"); s_op = mk("s_op")
    s_xs = mk("s_xs"); s_x2 = mk("s_x2"); s_sd = mk("s_sd"); s_ag2 = mk("s_ag2")
    s_sg = mk("s_sg"); s_bc = mk("s_bc"); s_ct = mk("s_ct"); s_it = mk("s_it"); s_ms = mk("s_ms")
    s_xt2 = mk("s_xt2"); s_w1 = mk("s_w1"); s_h = mk("s_h")
    s_g = mk("s_g"); s_w2 = mk("s_w2"); s_o2 = mk("s_o2"); s_tb = mk("s_tb")
    s_te = mk("s_te"); s_o2w = mk("s_o2w"); s_fi = mk("s_fi"); s_fe = mk("s_fe")
    s_out = mk("s_out"); s_qg = mk("s_qg"); s_cg = mk("s_cg"); s_qw = mk("s_qw")
    s_xp = [mk("s_xp0"), mk("s_xp1")]
    s_xwp = [mk("s_xwp0"), mk("s_xwp1")]
    s_qwp = [mk(f"s_qwp{i}") for i in range(4)]
    s_edp = [mk("s_edp0"), mk("s_edp1")]
    s_xsp = [mk("s_xsp0"), mk("s_xsp1")]
    s_w1p = [mk("s_w1p0"), mk("s_w1p1")]
    s_w2p = [mk("s_w2p0"), mk("s_w2p1")]
    s_owp = [mk("s_owp0"), mk("s_owp1")]
    s_fip = [mk("s_fip0"), mk("s_fip1")]
    s_oup = [mk("s_oup0"), mk("s_oup1")]
    s_wsp = [mk("s_wsp0"), mk("s_wsp1")]
    s_wop = [mk("s_wop0"), mk("s_wop1")]
    s_saa = mk("s_saa"); s_sta = mk("s_sta")
    s_qgp = [mk("s_qgp0"), mk("s_qgp1")]

    _block_cm = nc.Block()
    _the_blk = _block_cm.__enter__()

    class _Shim:
        @property
        def sync(self):
            return _the_blk.sync

        @property
        def gpsimd(self):
            return _the_blk.gpsimd

        @property
        def scalar(self):
            return _the_blk.scalar

        @property
        def vector(self):
            return _the_blk.vector

        @property
        def tensor(self):
            return _the_blk.tensor

    _blk_shim = _Shim()

    def vfl(vector):
        s_vf.n += 1
        vector.engine_nop().then_inc(s_vf.h, 1)
        vector.wait_ge(s_vf.h, s_vf.n)



    def finish(extra_waits=()):
        @_blk_shim.sync
        def _(sync):
            for sem, val in extra_waits:
                sync.wait_ge(sem.h, val)
            sync.wait_ge(s_out.h, s_out.n)
            sync.wait_ge(s_oup[0].h, s_oup[0].n)
            sync.wait_ge(s_oup[1].h, s_oup[1].n)
        _block_cm.__exit__(None, None, None)
        nc.all_engine_barrier()
        nc.clear_and_free_semaphores([s.h for s in sems])
        nc.all_engine_barrier()
        es.close()
        return nc

    att_es = ExitStack()
    kTb = SB(att_es, "kTb", [HD, NHL * S], BF16)
    vsb = SB(att_es, "vsb", [HD, NKT * QKW], BF16)

    if True:
        blk = _blk_shim
        @blk.sync
        def _(sync):
            for dst, src in ((ones_row[:], onesr_in[:]),
                             (ones_colf[:], ones_in[:]),
                             (tri_sb[:], tri_in[:]), (id_sb[:], id_in[:]),
                             (wr_sb[:], wr_in[:]), (br_sb[:], br_in[:]),
                             (b1_sb[:], b1c_in[:]), (b2c_sb[:], b2c_in[:]),
                             (agidx[:], agidx_in[:]), (scidx[:], scidx_in[:])):
                s_cst.inc(sync.dma_start(out=dst, in_=src), 16)

        @blk.gpsimd
        def _(gpsimd):
            s_cst.inc(gpsimd.memset(ones_colb[:], 1.0), 1)
    CST_ALL = s_cst.n

    b_es = ExitStack()
    wqb = SB(b_es, "wqb", [HD, DT_ * QKW], BF16)
    wkb = SB(b_es, "wkb", [HD, DT_ * QKW], BF16)
    wvb = SB(b_es, "wvb", [HD, DT_ * QKW], BF16)
    xt_f32 = SB(b_es, "xt_f32", [HD, 2 * D], F32)
    xln_t = SB(b_es, "xln_t", [HD, 2 * D], BF16)
    xlt = SB(b_es, "xlt", [HD, 2 * DT_ * CH], BF16)
    tab_sin = SB(b_es, "tab_sin", [HD, CH], F32)
    tab_cos = SB(b_es, "tab_cos", [HD, CH], F32)
    ang_buf = SB(b_es, "ang_buf", [HD, 2 * CH], F32)
    rtmp_s = SB(b_es, "rtmp_s", [HD, 2 * CH], F32)
    rtmp_d = SB(b_es, "rtmp_d", [HD, max(2 * CH, 2 * QKW)], F32)
    def rtmp_a(sl=slice(None)):
        return rtmp_d[:, CH:2 * CH]
    def rtmp_b(sl=slice(None)):
        return rtmp_s[:, CH:2 * CH]
    qstage = SB(b_es, "qstage", [HD, NHL * CH], BF16)
    qk_ps = [PS(b_es, f"qk_ps{i}", [HD, CH]) for i in range(4)]
    v_ps = [PS(b_es, f"v_ps{i}", [HD, QKW]) for i in range(2)]

    wstage = rtmp_d

    tpc = CH // HD
    stc = [0] * NCH

    def qkc(c, li):
        return c * (2 * NHL + GPH) + li + 1

    def qvc(c, t):
        return c * (2 * NHL + GPH) + 2 * NHL + t + 1

    def emit_chunk_transposes(sync, t):
        if (t + 1) % tpc != 0:
            return
        c = (t + 1) // tpc - 1
        cs = c * CH
        sync.wait_ge(s_xwp[t % 2].h, (t // 2 + 1) * 16)
        if t >= 1:
            sync.wait_ge(s_xwp[(t - 1) % 2].h, ((t - 1) // 2 + 1) * 16)
        if c >= 2:
            sync.wait_ge(s_qk.h, (2 * NHL + GPH) * (c - 1))
        xb = (c % 2) * DT_ * CH
        for dt_ in range(DT_):
            s_xtp[c % 2].inc(sync.dma_start(
                out=xlt[:, xb + dt_ * CH:xb + (dt_ + 1) * CH],
                in_=xln_hbm[cs:cs + CH, dt_ * HD:(dt_ + 1) * HD],
                transpose=True), 16)

    if True:
        blk = _blk_shim
        @blk.sync
        def _(sync):
            for g in range(NKT):
                if g >= 2:
                    sync.wait_ge(s_ln.h, g - 1)
                    s_xwp[g % 2].inc(sync.dma_start(
                        out=xln_hbm[(g - 2) * HD:(g - 1) * HD, :],
                        in_=xln_t[:, (g % 2) * D:(g % 2 + 1) * D]), 16)
                    emit_chunk_transposes(sync, g - 2)
                s_xp[g % 2].inc(sync.dma_start(
                    out=xt_f32[:, (g % 2) * D:(g % 2 + 1) * D],
                    in_=x_row[g * HD:(g + 1) * HD, :]), 16)
            for g in (NKT - 2, NKT - 1):
                sync.wait_ge(s_ln.h, g + 1)
                s_xwp[g % 2].inc(sync.dma_start(
                    out=xln_hbm[g * HD:(g + 1) * HD, :],
                    in_=xln_t[:, (g % 2) * D:(g % 2 + 1) * D]), 16)
                emit_chunk_transposes(sync, g)

        @blk.gpsimd
        def _(gpsimd):
            for c in range(NCH):
                cs = c * CH
                if c >= 1:
                    gpsimd.wait_ge(s_sta.h, c)
                for i, src in enumerate((ash_in, asl_in)):
                    s_ang.inc(gpsimd.dma_start(
                        out=ang_buf[:, i * CH:(i + 1) * CH],
                        in_=src[:, cs:cs + CH]), 16)
                gpsimd.wait_ge(s_saa.h, c + 1)
                for i, src in enumerate((ach_in, acl_in)):
                    s_ang.inc(gpsimd.dma_start(
                        out=ang_buf[:, i * CH:(i + 1) * CH],
                        in_=src[:, cs:cs + CH]), 16)

    def ln_tile_v(vector, g):
        p = g % 4
        vector.wait_ge(s_x.h, 2 * (g + 1))
        vector.tensor_scalar(out=small[:, 16 + p:17 + p], in0=small[:, 8 + p:9 + p],
                             scalar1=1.0 / D, scalar2=None, op0=ALU.mult)
        vector.tensor_scalar(out=small[:, 20 + p:21 + p], in0=small[:, 12 + p:13 + p],
                             scalar1=1.0 / D, scalar2=EPS, op0=ALU.mult,
                             op1=ALU.add)
        vfl(vector)
        vector.tensor_tensor(out=small[:, 28 + p:29 + p],
                             in0=small[:, 16 + p:17 + p],
                             in1=small[:, 16 + p:17 + p], op=ALU.mult)
        vfl(vector)
        vector.tensor_tensor(out=small[:, 20 + p:21 + p],
                             in0=small[:, 20 + p:21 + p],
                             in1=small[:, 28 + p:29 + p], op=ALU.subtract)
        s_xw.inc(vector.engine_nop())
        vector.wait_ge(s_sq.h, g + 1)
        vector.reciprocal(out=small[:, 24 + p:25 + p], in_=small[:, 28 + p:29 + p])
        vfl(vector)
        vector.scalar_tensor_tensor(out=small[:, 0 + p:1 + p],
                                    in0=small[:, 16 + p:17 + p], scalar=-1.0,
                                    in1=small[:, 24 + p:25 + p],
                                    op0=ALU.mult, op1=ALU.mult)
        vfl(vector)
        s_xt.inc(vector.engine_nop())

    def ln_tile_s(scalar, g):
        p = g % 4
        xv = xt_f32[:, (g % 2) * D:(g % 2 + 1) * D]
        scalar.wait_ge(s_xp[g % 2].h, (g // 2 + 1) * 16)
        scalar.activation(out=junkb[:], in_=xv, func=AF.Identity,
                          accum_out=small[:, 8 + p:9 + p])
        a = scalar.activation(out=junkb2[:], in_=xv, func=AF.Square,
                              accum_out=small[:, 12 + p:13 + p])
        s_x.inc(a)
        scalar.wait_ge(s_x.h, s_x.n)
        a = scalar.activation(out=junkb2[:, 0:1], in_=small[:, 8 + p:9 + p],
                              func=AF.Copy)
        s_x.inc(a)
        scalar.wait_ge(s_xw.h, g + 1)
        s_sq.inc(scalar.activation(out=small[:, 28 + p:29 + p],
                                   in_=small[:, 20 + p:21 + p], func=AF.Sqrt))
        scalar.wait_ge(s_xt.h, g + 1)
        if g >= 2:
            scalar.wait_ge(s_xwp[g % 2].h, (g // 2) * 16)
        a = scalar.activation(out=xln_t[:, (g % 2) * D:(g % 2 + 1) * D], in_=xv,
                              func=AF.Identity, bias=small[:, 0 + p:1 + p],
                              scale=small[:, 24 + p:25 + p])
        s_ln.inc(a)

    def emit_ln_chunk_v(c):
        @_blk_shim.vector
        def _(vector, c=c):
            for g in range(c * tpc, (c + 1) * tpc):
                ln_tile_v(vector, g)

    def emit_sq_chunk_s(c):
        @_blk_shim.scalar
        def _(scalar, c=c):
            for g in range(c * tpc, (c + 1) * tpc):
                ln_tile_s(scalar, g)

    npre = min(2, NCH)
    emit_ln_chunk_v(0)
    if NCH > 1:
        emit_ln_chunk_v(1)

    if True:
        blk = _blk_shim
        @blk.scalar
        def _(scalar):
            srcs = []
            for w_dst, w_src in ((wqb, wq_g), (wkb, wk_g), (wvb, wv_g)):
                for dt_ in range(DT_):
                    srcs.append((w_dst, w_src, dt_))
            nw = len(srcs)

            def ld(k):
                _, w_src, dt_ = srcs[k]
                s_wsp[k % 2].inc(scalar.dma_start(
                    out=wstage[:, (k % 2) * QKW:(k % 2 + 1) * QKW],
                    in_=w_src[dt_ * HD:(dt_ + 1) * HD, :]), 16)
            ld(0); ld(1)
            for k in range(nw):
                w_dst, _, dt_ = srcs[k]
                scalar.wait_ge(s_wsp[k % 2].h, (k // 2 + 1) * 16)
                a = scalar.activation(out=w_dst[:, dt_ * QKW:(dt_ + 1) * QKW],
                                      in_=wstage[:, (k % 2) * QKW:(k % 2 + 1) * QKW],
                                      func=AF.Copy)
                s_cg.inc(a)
                if k + 2 < nw:
                    scalar.wait_ge(s_cg.h, k + 1)
                    ld(k + 2)
    CG_W = s_cg.n
    emit_sq_chunk_s(0)
    if NCH > 1:
        emit_sq_chunk_s(1)

    for c in range(NCH):
        cs = c * CH
        if True:
            blk = _blk_shim
            @blk.vector
            def _(vector, c=c):
                if c >= 1:
                    vector.wait_ge(s_tab.h, c)
                    vfl(vector)
                else:
                    vector.wait_ge(s_cg.h, CG_W)
                vector.wait_ge(s_ang.h, c * 64 + 32)
                m = vector.tensor_tensor(out=rtmp_a(), in0=ang_buf[:, 0:CH],
                                         in1=ang_buf[:, CH:2 * CH], op=ALU.add)
                s_saa.inc(m)
                vector.wait_ge(s_ang.h, (c + 1) * 64)
                m = vector.tensor_tensor(out=rtmp_b(), in0=ang_buf[:, 0:CH],
                                         in1=ang_buf[:, CH:2 * CH], op=ALU.add)
                s_sta.inc(m)
                vfl(vector)
                vector.tensor_scalar(out=rtmp_a(), in0=rtmp_a(), scalar1=PCLIP,
                                     scalar2=-PCLIP, op0=ALU.min, op1=ALU.max)
                vector.tensor_scalar(out=rtmp_b(), in0=rtmp_b(), scalar1=PCLIP,
                                     scalar2=-PCLIP, op0=ALU.min, op1=ALU.max)
                vfl(vector)
                s_st.inc(vector.engine_nop())
                stc[c] = s_st.n

            @blk.scalar
            def _(scalar, c=c):
                scalar.wait_ge(s_st.h, stc[c])
                scalar.activation(out=tab_sin[:], in_=rtmp_a(), func=AF.Sin)
                s_tab.inc(scalar.activation(out=tab_cos[:], in_=rtmp_b(), func=AF.Sin))
                for n in range(NHL):
                    gi = c * (2 * NHL) + n * 2
                    scalar.wait_ge(s_rope.h, gi + 1)
                    s_qwp[n].inc(scalar.dma_start(
                        out=q_hbm[:, n * S + cs:n * S + cs + CH],
                        in_=qstage[:, n * CH:(n + 1) * CH]), 16)

            @blk.tensor
            def _(tensor, c=c, cs=cs):
                tensor.wait_ge(s_cg.h, CG_W)
                tensor.wait_ge(s_xtp[c % 2].h, (c // 2 + 1) * DT_ * 16)
                xb = (c % 2) * DT_ * CH
                for li in range(2 * NHL):
                    n, isk = li // 2, li % 2
                    gi = c * (2 * NHL) + li
                    pi = gi % 4
                    if gi >= 4:
                        tensor.wait_ge(s_rope.h, gi - 3)
                    wsb = wkb if isk else wqb
                    for dt_ in range(DT_):
                        m = tensor.matmul(qk_ps[pi][:],
                                          wsb[:, dt_ * QKW + n * HD:dt_ * QKW + (n + 1) * HD],
                                          xlt[:, xb + dt_ * CH:xb + (dt_ + 1) * CH],
                                          start=(dt_ == 0), stop=(dt_ == DT_ - 1))
                    s_qk.inc(m)
                for t in range(GPH):
                    vi = c * GPH + t
                    if vi >= 2:
                        tensor.wait_ge(s_vev.h, vi - 1)
                    for dt_ in range(DT_):
                        m = tensor.matmul(v_ps[vi % 2][:],
                                          xlt[:, xb + dt_ * CH + t * HD:xb + dt_ * CH + (t + 1) * HD],
                                          wvb[:, dt_ * QKW:(dt_ + 1) * QKW],
                                          start=(dt_ == 0), stop=(dt_ == DT_ - 1))
                    s_qk.inc(m)

            @blk.vector
            def _(vector, c=c, cs=cs):
                vector.wait_ge(s_tab.h, c + 1)
                for li in range(2 * NHL):
                    n, isk = li // 2, li % 2
                    gi = c * (2 * NHL) + li
                    pi = gi % 4
                    vector.wait_ge(s_qk.h, qkc(c, li))
                    if not isk and c >= 1:
                        vector.wait_ge(s_qwp[n].h, c * 16)
                    p = qk_ps[pi]
                    pr = (gi % 2) * CH
                    vector.stream_shuffle(out=rtmp_s[:, pr:pr + CH], in_=p[:], mask=SWAP32)
                    vector.tensor_tensor(out=rtmp_d[:, pr:pr + CH], in0=p[:],
                                         in1=tab_cos[:], op=ALU.mult)
                    vfl(vector)
                    vector.tensor_tensor(out=rtmp_s[:, pr:pr + CH],
                                         in0=rtmp_s[:, pr:pr + CH], in1=tab_sin[:],
                                         op=ALU.mult)
                    vfl(vector)
                    if isk:
                        vector.tensor_tensor(out=kTb[:, n * S + cs:n * S + cs + CH],
                                             in0=rtmp_d[:, pr:pr + CH],
                                             in1=rtmp_s[:, pr:pr + CH], op=ALU.add)
                    else:
                        vector.tensor_tensor(out=qstage[:, n * CH:(n + 1) * CH],
                                             in0=rtmp_d[:, pr:pr + CH],
                                             in1=rtmp_s[:, pr:pr + CH], op=ALU.add)
                    s_rope.inc(vector.engine_nop())
                for t in range(GPH):
                    vi = c * GPH + t
                    vector.wait_ge(s_qk.h, qvc(c, t))
                    vector.tensor_copy(out=vsb[:, vi * QKW:(vi + 1) * QKW],
                                       in_=v_ps[vi % 2][:])
                    s_vev.inc(vector.engine_nop())

        if c + 2 < NCH:
            emit_ln_chunk_v(c + 2)
            emit_sq_chunk_s(c + 2)

    LN_ALL = s_ln.n
    if "xln" in tap_t:
        if True:
            blk = _blk_shim
            @blk.sync
            def _(sync):
                sync.wait_ge(s_xwp[0].h, s_xwp[0].n)
                sync.wait_ge(s_xwp[1].h, s_xwp[1].n)
                s_out.inc(sync.dma_start(out=tap_t["xln"][:], in_=xln_hbm[:]), 16)

    ROPE_ALL = s_rope.n
    VEV_ALL = s_vev.n
    QK_ALL = s_qk.n
    if "qT" in tap_t or "kT" in tap_t or "v" in tap_t:
        if True:
            blk = _blk_shim
            @blk.sync
            def _(sync):
                sync.wait_ge(s_rope.h, ROPE_ALL)
                sync.wait_ge(s_vev.h, VEV_ALL)
                if "qT" in tap_t:
                    for _s in s_qwp:
                        sync.wait_ge(_s.h, _s.n)
                    s_out.inc(sync.dma_start(out=tap_t["qT"][:], in_=q_hbm[:]), 16)
                if "kT" in tap_t:
                    s_out.inc(sync.dma_start(out=tap_t["kT"][:], in_=kTb[:]), 16)
                if "v" in tap_t:
                    s_out.inc(sync.dma_start(out=tap_t["v"][:], in_=vsb[:]), 16)

    b_es.close()
    if stop_after == "qkv":
        att_es.close()
        return finish()

    p1_es = ExitStack()
    qg = SB(p1_es, "qg", [HD, 2 * CH], BF16)
    prob = SB(p1_es, "prob", [HD, 4 * CH], BF16)
    recip_sb = SB(p1_es, "recip_sb", [1, CH], F32)
    recipb_sb = SB(p1_es, "recipb_sb", [HD, CH], F32)
    astage = SB(p1_es, "astage", [HD, 2 * CH], BF16)
    att_ps = [PS(p1_es, f"att_ps{i}", [HD, CH]) for i in range(2)]
    lg_ps = [PS(p1_es, f"lg_ps{i}", [HD, CH]) for i in range(2)]
    rs_ps = [PS(p1_es, f"rs_ps{i}", [1, CH]) for i in range(2)]
    rb_ps = PS(p1_es, "rb_ps", [HD, CH])

    def _nk(gi):
        return GPH * (gi % NCH + 1)

    def _pre(gi):
        g = gi % NCH
        return (gi // NCH) * (GPH * NCH * (NCH + 1) // 2) + GPH * g * (g + 1) // 2

    def lgc(gi, j):
        return _pre(gi) + j + 1
    pvc = lgc

    _pex = [0]
    for _gi in range(NHL * NCH):
        _pex.append(_pex[-1] + (_nk(_gi) + 1) // 2)

    def PEX(gi):
        return _pex[gi]

    def exc2(gi, j):
        return PEX(gi) + j // 2 + 1

    def mkc(gi, j):
        return gi * GPH + (j - GPH * (gi % NCH)) + 1

    seq = [(n, g) for n in range(NHL) for g in range(NCH)]

    def qg_load(sync, gi):
        n, g = seq[gi]
        gq0 = g * CH
        sync.wait_ge(s_qwp[n].h, (g + 1) * 16)
        if gi <= 1:
            sync.wait_ge(s_qk.h, QK_ALL)
            sync.wait_ge(s_rope.h, ROPE_ALL)
            sync.wait_ge(s_vev.h, VEV_ALL)
        if gi >= 2:
            sync.wait_ge(s_lg.h, _pre(gi - 1))
        s_qgp[gi % 2].inc(sync.dma_start(out=qg[:, (gi % 2) * CH:(gi % 2 + 1) * CH],
                                         in_=q_hbm[:, n * S + gq0:n * S + gq0 + CH]), 16)

    def rb_bcast(tensor, gi):
        tensor.wait_ge(s_rc.h, gi + 1)
        if gi >= 1:
            tensor.wait_ge(s_ev.h, gi)
        m = tensor.matmul(rb_ps[:], ones_row[:], recip_sb[:],
                          start=True, stop=True)
        s_rb.inc(m)
    if True:
        blk = _blk_shim
        @blk.sync
        def _(sync):
            qg_load(sync, 0)

    for gi, (n, g) in enumerate(seq):
        nk = GPH * (g + 1)
        gq0 = g * CH
        if True:
            blk = _blk_shim
            @blk.tensor
            def _(tensor, n=n, g=g, gi=gi, nk=nk, gq0=gq0):
                if gi == 0:
                    tensor.wait_ge(s_rope.h, ROPE_ALL)
                    tensor.wait_ge(s_vev.h, VEV_ALL)
                    tensor.wait_ge(s_cst.h, CST_ALL)
                tensor.wait_ge(s_qgp[gi % 2].h, (gi // 2 + 1) * 16)

                def pv_j(j):
                    js = max(0, j - GPH * g) * HD
                    diag = j >= GPH * g
                    if diag:
                        tensor.wait_ge(s_mk.h, mkc(gi, j))
                    else:
                        tensor.wait_ge(s_ex.h, exc2(gi, j))
                    pslot = (j % 4) * CH
                    tensor.matmul(att_ps[gi % 2][:, js:CH],
                                  vsb[:, j * QKW + n * HD:j * QKW + (n + 1) * HD],
                                  prob[:, pslot + js:pslot + CH],
                                  start=(j == 0), stop=(j == nk - 1))
                    m = tensor.matmul(rs_ps[gi % 2][:, js:CH], ones_colb[:],
                                      prob[:, pslot + js:pslot + CH],
                                      start=(j == 0), stop=(j == nk - 1))
                    s_pv.inc(m)

                jpairs = [list(range(nk))[i:i + 2] for i in range(0, nk, 2)]
                for pp, pr in enumerate(jpairs):
                    if pp >= 1:
                        tensor.wait_ge(s_ex.h, PEX(gi) + pp)
                    elif gi >= 1:
                        tensor.wait_ge(s_ex.h, PEX(gi))
                    for j in pr:
                        js = max(0, j - GPH * g) * HD
                        m = tensor.matmul(lg_ps[j % 2][:, js:CH],
                                          kTb[:, n * S + j * HD:n * S + (j + 1) * HD],
                                          qg[:, (gi % 2) * CH + js:(gi % 2) * CH + CH],
                                          start=True, stop=True)
                        s_lg.inc(m)
                    if pp == 0:
                        if gi >= 2:
                            tensor.wait_ge(s_ev.h, gi - 1)
                        if gi >= 1:
                            rb_bcast(tensor, gi - 1)
                    if pp >= 1:
                        for j in jpairs[pp - 1]:
                            pv_j(j)
                for j in jpairs[-1]:
                    pv_j(j)
                if gi == len(seq) - 1:
                    rb_bcast(tensor, gi)

            @blk.scalar
            def _(scalar, n=n, g=g, gi=gi, nk=nk):
                jpairs = [list(range(nk))[i:i + 2] for i in range(0, nk, 2)]
                for pp, pr in enumerate(jpairs):
                    if pp >= 2:
                        scalar.wait_ge(s_pv.h, pvc(gi, pr[-1] - 4))
                    elif gi >= 1 and pp == 0:
                        scalar.wait_ge(s_pv.h, _pre(gi))
                    scalar.wait_ge(s_lg.h, lgc(gi, pr[-1]))
                    for j in pr:
                        js = max(0, j - GPH * g) * HD
                        pslot = (j % 4) * CH
                        a = scalar.activation(out=prob[:, pslot + js:pslot + CH],
                                              in_=lg_ps[j % 2][:, js:CH],
                                              func=AF.Exp, scale=SCALE)
                    s_ex.inc(a)

            @blk.vector
            def _(vector, n=n, g=g, gi=gi, nk=nk):
                for j in range(GPH * g, nk):
                    js = (j - GPH * g) * HD
                    pslot = (j % 4) * CH
                    vector.wait_ge(s_ex.h, exc2(gi, j))
                    m = vector.tensor_tensor(out=prob[:, pslot + js:pslot + js + HD],
                                             in0=prob[:, pslot + js:pslot + js + HD],
                                             in1=tri_sb[:], op=ALU.mult)
                    s_mk.inc(m)
                vector.wait_ge(s_pv.h, pvc(gi, nk - 1))
                vector.reciprocal(out=recip_sb[:], in_=rs_ps[gi % 2][:])
                vfl(vector)
                s_rc.inc(vector.engine_nop())
                vector.wait_ge(s_rb.h, gi + 1)
                if gi >= 2:
                    vector.wait_ge(s_edp[gi % 2].h, (gi // 2) * 16)
                vector.tensor_copy(out=recipb_sb[:], in_=rb_ps[:])
                vfl(vector)
                m = vector.tensor_tensor(out=astage[:, (gi % 2) * CH:(gi % 2 + 1) * CH],
                                         in0=att_ps[gi % 2][:], in1=recipb_sb[:],
                                         op=ALU.mult)
                s_ev.inc(m)

            @blk.sync
            def _(sync, n=n, g=g, gi=gi, gq0=gq0):
                if gi + 1 < len(seq):
                    qg_load(sync, gi + 1)
                sync.wait_ge(s_ev.h, gi + 1)
                ag_t = agin1a if n < HHL else agin1b
                nn = n if n < HHL else n - HHL
                s_edp[gi % 2].inc(sync.dma_start(
                    out=ag_t[nn * HD:(nn + 1) * HD, gq0:gq0 + CH],
                    in_=astage[:, (gi % 2) * CH:(gi % 2 + 1) * CH]), 16)

    if "attnT" in tap_t:
        if True:
            blk = _blk_shim
            @blk.sync
            def _(sync):
                sync.wait_ge(s_edp[0].h, s_edp[0].n)
                sync.wait_ge(s_edp[1].h, s_edp[1].n)
                s_out.inc(sync.dma_start(out=tap_t["attnT"][:], in_=agin1a[:]), 16)

    if True:
        blk = _blk_shim
        @blk.gpsimd
        def _(gpsimd):
            nga = HHL * NCH
            gpsimd.wait_ge(s_edp[0].h, (nga - nga // 2) * 16)
            gpsimd.wait_ge(s_edp[1].h, (nga // 2) * 16)
            s_ag1.inc(gpsimd.collective_compute(
                "AllGather", ALU.bypass,
                replica_groups=[[0, 1, 2, 3, 4, 5, 6, 7]],
                ins=[agin1a.ap().opt()], outs=[agout1a.ap().opt()]), 1)
            gpsimd.wait_ge(s_edp[0].h, s_edp[0].n)
            gpsimd.wait_ge(s_edp[1].h, s_edp[1].n)
            s_ag1.inc(gpsimd.collective_compute(
                "AllGather", ALU.bypass,
                replica_groups=[[0, 1, 2, 3, 4, 5, 6, 7]],
                ins=[agin1b.ap().opt()], outs=[agout1b.ap().opt()]), 1)

    EV_ALL = s_ev.n
    ED0_ALL = s_edp[0].n
    ED1_ALL = s_edp[1].n
    p1_es.close()
    att_es.close()
    if stop_after == "attn":
        return finish(extra_waits=[(s_ag1, 2)])

    p3_es = ExitStack()
    wob = SB(p3_es, "wob", [HD, NH * D], BF16)
    wostage = SB(p3_es, "wostage", [HD, 2 * D], F32)
    atm = SB(p3_es, "atm", [HD, NH * TS], BF16)
    wrb = SB(p3_es, "wrb", [HD, D], F32)
    scr_f32 = SB(p3_es, "scr_f32", [HD, D], F32)
    xs_t = SB(p3_es, "xs_t", [HD, 2 * D], F32)
    x2st = SB(p3_es, "x2st", [HD, 2 * D], F32)
    x2lst = SB(p3_es, "x2lst", [HD, 2 * D], BF16)
    junkb3 = SB(p3_es, "junkb3", [HD, D], BF16)
    op_ps = [[PS(p3_es, f"op_ps{i}_{j}", [HD, CH]) for j in range(D // CH)]
             for i in range(2)]

    if True:
        blk = _blk_shim
        @blk.sync
        def _(sync):
            for nt in range(NH):
                if nt == 0:
                    sync.wait_ge(s_ev.h, EV_ALL)
                    sync.wait_ge(s_edp[0].h, ED0_ALL)
                    sync.wait_ge(s_edp[1].h, ED1_ALL)
                elif nt >= 2:
                    sync.wait_ge(s_wo.h, nt - 1)
                s_wop[nt % 2].inc(sync.dma_start(
                    out=wostage[:, (nt % 2) * D:(nt % 2 + 1) * D],
                    in_=wo_in[nt * HD:(nt + 1) * HD, :]), 16)

        @blk.scalar
        def _(scalar):
            for nt in range(NH):
                scalar.wait_ge(s_wop[nt % 2].h, (nt // 2 + 1) * 16)
                a = scalar.activation(out=wob[:, nt * D:(nt + 1) * D],
                                      in_=wostage[:, (nt % 2) * D:(nt % 2 + 1) * D],
                                      func=AF.Copy)
                s_wo.inc(a)

        atm_order = ([i for i in range(NH) if i % NHL < HHL] +
                     [i for i in range(NH) if i % NHL >= HHL])

        @blk.gpsimd
        def _(gpsimd):
            for j, i in enumerate(atm_order):
                gpsimd.wait_ge(s_ag1.h, 1 if i % NHL < HHL else 2)
                if j >= 1:
                    gpsimd.wait_ge(s_am.h, j * 16)
                src_t = agout1a if i % NHL < HHL else agout1b
                s_am.inc(gpsimd.indirect_dma_start(
                    out=atm[:, i * TS:(i + 1) * TS],
                    out_offset=None,
                    in_=src_t[:, :],
                    in_offset=IndirectOffsetOnAxis(ap=agidx[:, i:i + 1], axis=0)), 16)

        @blk.tensor
        def _(tensor):
            tensor.wait_ge(s_cst.h, CST_ALL)
            tensor.wait_ge(s_ev.h, EV_ALL)
            for ch2 in range(D // CH):
                m = tensor.matmul(op_ps[0][ch2][:], ones_row[:],
                                  wr_sb[:, ch2 * CH:(ch2 + 1) * CH],
                                  start=True, stop=True)
            s_bc.inc(m)

        @blk.vector
        def _(vector):
            vector.wait_ge(s_bc.h, 1)
            for ch2 in range(D // CH):
                vector.tensor_copy(out=wrb[:, ch2 * CH:(ch2 + 1) * CH],
                                   in_=op_ps[0][ch2][:])
            vfl(vector)
            s_xs.inc(vector.engine_nop())

    if stop_after == "p3a":
        p3_es.close()
        return finish(extra_waits=[(s_wo, NH), (s_am, NH * 16), (s_xs, 1)])

    SQ_P3 = s_sq.n
    TAB_P3 = s_tab.n
    ST_P3 = s_st.n
    for t in range(NTT):
        if True:
            blk = _blk_shim
            @blk.sync
            def _(sync, t=t):
                if t < 2:
                    sync.wait_ge(s_ev.h, EV_ALL)
                    sync.wait_ge(s_edp[t % 2].h, (ED0_ALL, ED1_ALL)[t % 2])
                else:
                    sync.wait_ge(s_x2.h, t - 1)
                s_xsp[t % 2].inc(sync.dma_start(
                    out=xs_t[:, (t % 2) * D:(t % 2 + 1) * D],
                    in_=x_sl[t * HD:(t + 1) * HD, :]), 16)

            @blk.tensor
            def _(tensor, t=t):
                tensor.wait_ge(s_wo.h, NH)
                if t == 0:
                    tensor.wait_ge(s_xs.h, 1)
                if t == 1:
                    tensor.wait_ge(s_ev.h, EV_ALL)
                if t >= 2:
                    tensor.wait_ge(s_x2.h, t - 1)
                for j, nt in enumerate(atm_order):
                    tensor.wait_ge(s_am.h, (j + 1) * 16)
                    for ch2 in range(D // CH):
                        m = tensor.matmul(
                            op_ps[t % 2][ch2][:],
                            atm[:, nt * TS + t * HD:nt * TS + (t + 1) * HD],
                            wob[:, nt * D + ch2 * CH:nt * D + (ch2 + 1) * CH],
                            start=(j == 0), stop=(j == NH - 1))
                s_op.inc(m)

            @blk.vector
            def _(vector, t=t):
                p = t % 4
                sl2 = (t % 2) * D
                vector.wait_ge(s_op.h, t + 1)
                vector.wait_ge(s_xsp[t % 2].h, (t // 2 + 1) * 16)
                for ch2 in range(D // CH):
                    vector.tensor_tensor(out=x2st[:, sl2 + ch2 * CH:sl2 + (ch2 + 1) * CH],
                                         in0=op_ps[t % 2][ch2][:],
                                         in1=xs_t[:, sl2 + ch2 * CH:sl2 + (ch2 + 1) * CH],
                                         op=ALU.add)
                vfl(vector)
                s_st.inc(vector.engine_nop())

            @blk.scalar
            def _(scalar, t=t):
                p = t % 4
                sl2 = (t % 2) * D
                x2v = x2st[:, sl2:sl2 + D]
                scalar.wait_ge(s_st.h, ST_P3 + t + 1)
                scalar.activation(out=junkb2[:], in_=x2v, func=AF.Identity,
                                  accum_out=small[:, 8 + p:9 + p])
                a = scalar.activation(out=junkb3[:], in_=x2v, func=AF.Square,
                                      accum_out=small[:, 12 + p:13 + p])
                s_fi.inc(a)
                scalar.wait_ge(s_fi.h, s_fi.n)
                a = scalar.activation(out=junkb3[:, 0:1], in_=small[:, 8 + p:9 + p],
                                      func=AF.Copy)
                s_fi.inc(a)

            @blk.vector
            def _(vector, t=t):
                p = t % 4
                vector.wait_ge(s_fi.h, 2 * (t + 1))
                vector.tensor_scalar(out=small[:, 16 + p:17 + p], in0=small[:, 8 + p:9 + p],
                                     scalar1=1.0 / D, scalar2=None, op0=ALU.mult)
                vector.tensor_scalar(out=small[:, 20 + p:21 + p], in0=small[:, 12 + p:13 + p],
                                     scalar1=1.0 / D, scalar2=EPS, op0=ALU.mult,
                                     op1=ALU.add)
                vfl(vector)
                vector.tensor_tensor(out=small[:, 28 + p:29 + p],
                                     in0=small[:, 16 + p:17 + p],
                                     in1=small[:, 16 + p:17 + p], op=ALU.mult)
                vfl(vector)
                vector.tensor_tensor(out=small[:, 20 + p:21 + p],
                                     in0=small[:, 20 + p:21 + p],
                                     in1=small[:, 28 + p:29 + p], op=ALU.subtract)
                s_w1.inc(vector.engine_nop())

            @blk.scalar
            def _(scalar, t=t):
                p = t % 4
                scalar.wait_ge(s_w1.h, t + 1)
                s_g.inc(scalar.activation(out=small[:, 28 + p:29 + p],
                                          in_=small[:, 20 + p:21 + p], func=AF.Sqrt))

            @blk.vector
            def _(vector, t=t):
                p = t % 4
                vector.wait_ge(s_g.h, t + 1)
                vector.reciprocal(out=small[:, 24 + p:25 + p], in_=small[:, 28 + p:29 + p])
                vfl(vector)
                vector.scalar_tensor_tensor(out=small[:, 0 + p:1 + p],
                                            in0=small[:, 16 + p:17 + p], scalar=-1.0,
                                            in1=small[:, 24 + p:25 + p],
                                            op0=ALU.mult, op1=ALU.mult)
                vector.tensor_scalar(out=small[:, 28 + p:29 + p],
                                     in0=small[:, 16 + p:17 + p],
                                     scalar1=-1.0, scalar2=None, op0=ALU.mult)
                vfl(vector)
                s_sq.inc(vector.engine_nop())

            @blk.scalar
            def _(scalar, t=t):
                p = t % 4
                sl2 = (t % 2) * D
                x2v = x2st[:, sl2:sl2 + D]
                scalar.wait_ge(s_sq.h, SQ_P3 + t + 1)
                if t >= 1:
                    scalar.wait_ge(s_x2.h, t)
                scalar.activation(out=x2lst[:, sl2:sl2 + D], in_=x2v, func=AF.Identity,
                                  bias=small[:, 0 + p:1 + p],
                                  scale=small[:, 24 + p:25 + p])
                a = scalar.activation(out=scr_f32[:], in_=x2v, func=AF.Identity,
                                      bias=small[:, 28 + p:29 + p], scale=1.0)
                s_tab.inc(a)

            @blk.vector
            def _(vector, t=t):
                p = t % 4
                vector.wait_ge(s_tab.h, TAB_P3 + t + 1)
                vector.scalar_tensor_tensor(out=junkb[:], in0=scr_f32[:], scalar=1.0,
                                            in1=wrb[:], op0=ALU.mult, op1=ALU.mult,
                                            accum_out=small[:, 4 + p:5 + p])
                vfl(vector)
                vector.tensor_tensor(out=sc_mine[:, t:t + 1], in0=small[:, 4 + p:5 + p],
                                     in1=small[:, 24 + p:25 + p], op=ALU.mult)
                vfl(vector)
                s_x2.inc(vector.engine_nop())

            @blk.sync
            def _(sync, t=t):
                sync.wait_ge(s_x2.h, t + 1)
                sync.wait_ge(s_tab.h, TAB_P3 + t + 1)
                sl2 = (t % 2) * D
                s_sd.inc(sync.dma_start(out=x2_hbm[t * HD:(t + 1) * HD, :],
                                        in_=x2st[:, sl2:sl2 + D]), 16)
                s_sd.inc(sync.dma_start(out=x2ln_hbm[t * HD:(t + 1) * HD, :],
                                        in_=x2lst[:, sl2:sl2 + D]), 16)

    if stop_after == "p3b":
        p3_es.close()
        return finish(extra_waits=[(s_x2, NTT), (s_sd, s_sd.n)])

    if True:
        blk = _blk_shim
        @blk.tensor
        def _(tensor):
            tensor.wait_ge(s_cst.h, CST_ALL)
            tensor.wait_ge(s_x2.h, NTT)
            m = tensor.matmul(op_ps[0][0][:, 0:1], ones_row[:], br_sb[:], start=True, stop=True)
            s_bc.inc(m)

        @blk.vector
        def _(vector):
            vector.wait_ge(s_bc.h, 2)
            vector.tensor_copy(out=small[:, 7:8], in_=op_ps[0][0][:, 0:1])
            vfl(vector)
            vector.tensor_scalar(out=sc_mine[:], in0=sc_mine[:],
                                 scalar1=small[:, 7:8], scalar2=None, op0=ALU.add)
            vfl(vector)
            s_x2.inc(vector.engine_nop())

        @blk.sync
        def _(sync):
            sync.wait_ge(s_x2.h, s_x2.n)
            for t in range(NTT):
                s_sd.inc(sync.dma_start(
                    out=agin2.rearrange("a (t p) -> (a t) p", p=HD)[t:t + 1, :],
                    in_=sc_mine[:, t:t + 1]), 16)

    SD_ALL = s_sd.n
    if True:
        blk = _blk_shim
        @blk.gpsimd
        def _(gpsimd):
            gpsimd.wait_ge(s_sd.h, SD_ALL)
            s_ag2.inc(gpsimd.collective_compute(
                "AllGather", ALU.bypass,
                replica_groups=[[0, 1, 2, 3, 4, 5, 6, 7]],
                ins=[agin2.ap().opt()], outs=[agout2.ap().opt()]), 1)

    p3_es.close()
    if stop_after == "proj":
        return finish(extra_waits=[(s_ag2, 1)])

    ffn_es = ExitStack()
    hT = SB(ffn_es, "hT", [HD, FSL * TS], BF16)
    sc_g = SB(ffn_es, "sc_g", [HD, S // HD], F32)

    f1_es = ExitStack()
    x2lt = SB(f1_es, "x2lt", [HD, DT_ * TS], BF16)
    w1b = [SB(f1_es, f"w1b{i}", [HD, DT_ * HD], BF16) for i in range(2)]
    w1stage = [SB(f1_es, f"w1stage{i}", [HD, DT_ * HD], F32) for i in range(2)]
    h_ps = [[PS(f1_es, f"h_ps{i}_{j}", [HD, CH]) for j in range(TS // CH)] for i in range(2)]
    thr_ps = PS(f1_es, "thr_ps", [HD, 1])
    cnt_ps = PS(f1_es, "cnt_ps", [1, 1])

    BC_B = s_bc.n
    CT_B = s_ct.n
    IT_B = s_it.n
    G_B = s_g.n
    QW_B = s_qw.n
    if True:
        blk = _blk_shim
        @blk.gpsimd
        def _(gpsimd):
            gpsimd.wait_ge(s_ag2.h, 1)
            s_sg.inc(gpsimd.indirect_dma_start(
                out=sc_g[:], out_offset=None, in_=agout2[:, :],
                in_offset=IndirectOffsetOnAxis(ap=scidx[:, 0:1], axis=0)), 16)

        @blk.sync
        def _(sync):
            sync.wait_ge(s_sd.h, SD_ALL)
            for dt_ in range(DT_):
                s_xt2.inc(sync.dma_start(
                    out=x2lt[:, dt_ * TS:(dt_ + 1) * TS],
                    in_=x2ln_hbm[:, dt_ * HD:(dt_ + 1) * HD],
                    transpose=True), 16)

        @blk.vector
        def _(vector):
            vector.wait_ge(s_sg.h, 16)
            vector.wait_ge(s_cst.h, CST_ALL)
            vector.tensor_scalar(out=tiny[:, 4:5], in0=ones_row[0:1, 0:1],
                                 scalar1=-1e4, scalar2=None, op0=ALU.mult)
            vector.tensor_scalar(out=tiny[:, 5:6], in0=ones_row[0:1, 0:1],
                                 scalar1=1e4, scalar2=None, op0=ALU.mult)
            vfl(vector)
            s_it.inc(vector.engine_nop())
            for it in range(N_ITER):
                vector.tensor_tensor(out=tiny[:, 6:7], in0=tiny[:, 4:5],
                                     in1=tiny[:, 5:6], op=ALU.add)
                vfl(vector)
                vector.tensor_scalar(out=tiny[:, 6:7], in0=tiny[:, 6:7], scalar1=0.5,
                                     scalar2=None, op0=ALU.mult)
                vfl(vector)
                s_it.inc(vector.engine_nop())
                vector.wait_ge(s_bc.h, BC_B + it + 1)
                vector.tensor_scalar(out=junkb[:, 0:S // HD], in0=sc_g[:],
                                     scalar1=thr_ps[:, 0:1], scalar2=0.0,
                                     op0=ALU.is_gt, op1=ALU.add,
                                     accum_out=small[:, 30:31])
                vfl(vector)
                s_it.inc(vector.engine_nop())
                vector.wait_ge(s_ct.h, CT_B + it + 1)
                vector.tensor_single_scalar(out=tiny_i[:, 0:1], in_=cnt_ps[:],
                                            scalar=float(KSEL), op=ALU.is_ge)
                vector.tensor_single_scalar(out=tiny_i[:, 1:2], in_=cnt_ps[:],
                                            scalar=float(KSEL), op=ALU.is_lt)
                vfl(vector)
                vector.copy_predicated(out=tiny[:, 4:5], mask=tiny_i[:, 0:1],
                                       data=tiny[:, 6:7])
                vector.copy_predicated(out=tiny[:, 5:6], mask=tiny_i[:, 1:2],
                                       data=tiny[:, 6:7])
                vfl(vector)
            s_it.inc(vector.engine_nop())
            vector.wait_ge(s_bc.h, BC_B + N_ITER + 1)
            vector.tensor_scalar(out=mask_sb[:], in0=sc_mine[:],
                                 scalar1=thr_ps[:, 0:1], scalar2=0.0,
                                 op0=ALU.is_gt, op1=ALU.add)
            vfl(vector)
            s_ms.inc(vector.engine_nop())

    if "scg" in tap_t or "thr" in tap_t or "sc" in tap_t:
        if True:
            blk = _blk_shim
            @blk.sync
            def _(sync):
                sync.wait_ge(s_ms.h, 1)
                if "scg" in tap_t:
                    s_out.inc(sync.dma_start(out=tap_t["scg"][:], in_=sc_g[:]), 16)
                if "thr" in tap_t:
                    s_out.inc(sync.dma_start(out=tap_t["thr"][:], in_=tiny[:, 4:8]), 16)
                if "sc" in tap_t:
                    s_out.inc(sync.dma_start(out=tap_t["sc"][:], in_=agin2[:]), 16)
    if "x2" in tap_t or "x2ln" in tap_t:
        if True:
            blk = _blk_shim
            @blk.sync
            def _(sync):
                sync.wait_ge(s_sd.h, SD_ALL)
                if "x2" in tap_t:
                    s_out.inc(sync.dma_start(out=tap_t["x2"][:], in_=x2_hbm[:]), 16)
                if "x2ln" in tap_t:
                    s_out.inc(sync.dma_start(out=tap_t["x2ln"][:], in_=x2ln_hbm[:]), 16)

    if True:
        blk = _blk_shim
        @blk.sync
        def _(sync):
            for f in range(FSL):
                if f < 2:
                    sync.wait_ge(s_op.h, NTT)
                else:
                    sync.wait_ge(s_h.h, f - 1)
                s_w1p[f % 2].inc(sync.dma_start(
                    out=w1stage[f % 2][:],
                    in_=w1_in[f * HD:(f + 1) * HD, :]), 16)

        @blk.tensor
        def _(tensor):
            tensor.wait_ge(s_cst.h, CST_ALL)
            tensor.wait_ge(s_xt2.h, DT_ * 16)
            S_OFF = 8 if FSL >= N_ITER + 8 else 0
            for f in range(FSL):
                if S_OFF <= f < N_ITER + S_OFF:
                    it = f - S_OFF
                    tensor.wait_ge(s_it.h, IT_B + 2 + 2 * it)
                    s_bc.inc(tensor.matmul(thr_ps[:], ones_row[:], tiny[:, 6:7],
                                            start=True, stop=True))
                    tensor.wait_ge(s_it.h, IT_B + 3 + 2 * it)
                    s_ct.inc(tensor.matmul(cnt_ps[:], small[:, 30:31], ones_colf[:],
                                           start=True, stop=True))
                elif f == N_ITER + S_OFF:
                    tensor.wait_ge(s_it.h, IT_B + 2 + 2 * N_ITER)
                    s_bc.inc(tensor.matmul(thr_ps[:], ones_row[:], tiny[:, 4:5],
                                            start=True, stop=True))
                tensor.wait_ge(s_qw.h, QW_B + f + 1)
                if f >= 2:
                    tensor.wait_ge(s_g.h, G_B + f - 1)
                for ch2 in range(TS // CH):
                    for dt_ in range(DT_):
                        m = tensor.matmul(
                            h_ps[f % 2][ch2][:],
                            w1b[f % 2][:, dt_ * HD:(dt_ + 1) * HD],
                            x2lt[:, dt_ * TS + ch2 * CH:dt_ * TS + (ch2 + 1) * CH],
                            start=(dt_ == 0), stop=(dt_ == DT_ - 1))
                s_h.inc(m)
            for it in range(max(0, min(FSL - S_OFF, N_ITER)), N_ITER):
                tensor.wait_ge(s_it.h, IT_B + 2 + 2 * it)
                s_bc.inc(tensor.matmul(thr_ps[:], ones_row[:], tiny[:, 6:7],
                                       start=True, stop=True))
                tensor.wait_ge(s_it.h, IT_B + 3 + 2 * it)
                s_ct.inc(tensor.matmul(cnt_ps[:], small[:, 30:31], ones_colf[:],
                                       start=True, stop=True))
            if FSL <= N_ITER + S_OFF:
                tensor.wait_ge(s_it.h, IT_B + 2 + 2 * N_ITER)
                s_bc.inc(tensor.matmul(thr_ps[:], ones_row[:], tiny[:, 4:5],
                                       start=True, stop=True))

        @blk.scalar
        def _(scalar):
            def gelu(f):
                scalar.wait_ge(s_h.h, f + 1)
                for ch2 in range(TS // CH):
                    a = scalar.activation(
                        out=hT[:, f * TS + ch2 * CH:f * TS + (ch2 + 1) * CH],
                        in_=h_ps[f % 2][ch2][:],
                        func=AF.Gelu_apprx_tanh, bias=b1_sb[:, f:f + 1], scale=1.0)
                s_g.inc(a)
            for f in range(FSL):
                scalar.wait_ge(s_w1p[f % 2].h, (f // 2 + 1) * 16)
                if f >= 2:
                    scalar.wait_ge(s_h.h, f - 1)
                a = scalar.activation(out=w1b[f % 2][:], in_=w1stage[f % 2][:],
                                      func=AF.Copy)
                s_qw.inc(a)
                if f >= 1:
                    gelu(f - 1)
            gelu(FSL - 1)

    H_ALL = s_h.n
    G_ALL = s_g.n
    if stop_after == "w1":
        f1_es.close()
        ffn_es.close()
        return finish(extra_waits=[(s_g, G_ALL)])
    if "hT" in tap_t:
        if True:
            blk = _blk_shim
            @blk.sync
            def _(sync):
                sync.wait_ge(s_g.h, G_ALL)
                s_out.inc(sync.dma_start(out=tap_t["hT"][:], in_=hT[:, 0:min(16, FSL) * TS]), 16)

    f1_es.close()

    f2_es = ExitStack()
    w2b = [SB(f2_es, f"w2b{i}", [HD, FSL * HD], BF16) for i in range(2)]
    w2stage = [SB(f2_es, f"w2stage{i}", [HD, FSL * HD // max(1, FSL // 16)], F32)
               for i in range(2)]
    o2fb = [SB(f2_es, f"o2fb{i}", [HD, TS], BF16) for i in range(2)]
    o2_ps = [[PS(f2_es, f"o2_ps{i}_{j}", [HD, CH]) for j in range(TS // CH)] for i in range(2)]

    CT_W2 = s_ct.n

    if True:
        blk = _blk_shim
        @blk.scalar
        def _(scalar):
            nsub = max(1, FSL // 16)
            fsub = FSL // nsub
            nld = DSL * nsub
            scalar.wait_ge(s_ct.h, CT_W2)

            def ld(k):
                ds_, sb2 = divmod(k, nsub)
                s_w2p[k % 2].inc(scalar.dma_start(
                    out=w2stage[k % 2][:],
                    in_=w2_in[ds_ * HD:(ds_ + 1) * HD,
                              sb2 * fsub * HD:(sb2 + 1) * fsub * HD]), 16)
            scalar.wait_ge(s_h.h, FSL)
            ld(0); ld(1)
            for ds in range(DSL):
                for sb_ in range(nsub):
                    k = ds * nsub + sb_
                    scalar.wait_ge(s_w2p[k % 2].h, (k // 2 + 1) * 16)
                    if ds >= 2 and sb_ == 0:
                        scalar.wait_ge(s_o2.h, ds - 1)
                    a = scalar.activation(
                        out=w2b[ds % 2][:, sb_ * fsub * HD:(sb_ + 1) * fsub * HD],
                        in_=w2stage[k % 2][:], func=AF.Copy)
                    s_ct.inc(a)
                    if k + 2 < nld:
                        scalar.wait_ge(s_ct.h, CT_W2 + k + 1)
                        ld(k + 2)

        @blk.tensor
        def _(tensor):
            tensor.wait_ge(s_g.h, G_ALL)
            for ds in range(DSL):
                tensor.wait_ge(s_ct.h, CT_W2 + (ds + 1) * max(1, FSL // 16))
                if ds >= 2:
                    tensor.wait_ge(s_te.h, ds - 1)
                for ft in range(FSL):
                    for ch2 in range(TS // CH):
                        m = tensor.matmul(
                            o2_ps[ds % 2][ch2][:],
                            w2b[ds % 2][:, ft * HD:(ft + 1) * HD],
                            hT[:, ft * TS + ch2 * CH:ft * TS + (ch2 + 1) * CH],
                            start=(ft == 0), stop=(ft == FSL - 1))
                s_o2.inc(m)

        @blk.vector
        def _(vector):
            for ds in range(DSL):
                vector.wait_ge(s_o2.h, ds + 1)
                if ds >= 2:
                    vector.wait_ge(s_owp[ds % 2].h, (ds // 2) * 16)
                for ch2 in range(TS // CH):
                    vector.tensor_scalar(out=o2fb[ds % 2][:, ch2 * CH:(ch2 + 1) * CH],
                                         in0=o2_ps[ds % 2][ch2][:],
                                         scalar1=b2c_sb[:, ds:ds + 1], scalar2=None,
                                         op0=ALU.add)
                vfl(vector)
                s_te.inc(vector.engine_nop())

        @blk.sync
        def _(sync):
            for ds in range(DSL):
                sync.wait_ge(s_te.h, ds + 1)
                s_owp[ds % 2].inc(sync.dma_start(
                    out=o2_hbm[ds * HD:(ds + 1) * HD, :],
                    in_=o2fb[ds % 2][:]), 16)

    f2_es.close()
    if stop_after == "w2":
        ffn_es.close()
        return finish(extra_waits=[(s_owp[0], s_owp[0].n), (s_owp[1], s_owp[1].n)])
    if "o2" in tap_t:
        if True:
            blk = _blk_shim
            @blk.sync
            def _(sync):
                sync.wait_ge(s_owp[0].h, s_owp[0].n)
                sync.wait_ge(s_owp[1].h, s_owp[1].n)
                s_out.inc(sync.dma_start(out=tap_t["o2"][:], in_=o2_hbm[:]), 16)

    ep_es = ExitStack()
    oin = SB(ep_es, "oin", [HD, 2 * D], BF16)
    x2in = SB(ep_es, "x2in", [HD, 2 * D], F32)
    ostage = SB(ep_es, "ostage", [HD, 4 * D], F32)
    if True:
        blk = _blk_shim
        @blk.sync
        def _(sync):
            sync.wait_ge(s_owp[0].h, s_owp[0].n)
            sync.wait_ge(s_owp[1].h, s_owp[1].n)
            def wr(t):
                sl4 = (t % 4) * D
                sync.wait_ge(s_fe.h, t + 1)
                s_oup[t % 2].inc(sync.dma_start(out=out_sl[t * HD:(t + 1) * HD, :],
                                                in_=ostage[:, sl4:sl4 + D]), 16)
            for t in range(NTT):
                sl2 = (t % 2) * D
                if t >= 2:
                    sync.wait_ge(s_fe.h, t - 1)
                s_fip[t % 2].inc(sync.dma_start(out=oin[:, sl2:sl2 + D],
                                                in_=o2_hbm[:, t * HD:(t + 1) * HD],
                                                transpose=True), 16)
                if t >= 2:
                    wr(t - 2)
            wr(NTT - 2)
            wr(NTT - 1)

        @blk.scalar
        def _(scalar):
            scalar.wait_ge(s_ms.h, 1)
            scalar.wait_ge(s_owp[0].h, s_owp[0].n)
            scalar.wait_ge(s_owp[1].h, s_owp[1].n)
            for t in range(NTT):
                sl2 = (t % 2) * D
                if t >= 2:
                    scalar.wait_ge(s_fe.h, t - 1)
                s_fip[t % 2].inc(scalar.dma_start(out=x2in[:, sl2:sl2 + D],
                                                  in_=x2_hbm[t * HD:(t + 1) * HD, :]), 16)

        @blk.vector
        def _(vector):
            vector.wait_ge(s_ms.h, 1)
            for t in range(NTT):
                sl2 = (t % 2) * D
                sl4 = (t % 4) * D
                vector.wait_ge(s_fip[t % 2].h, (t // 2 + 1) * 32)
                if t >= 4:
                    vector.wait_ge(s_oup[t % 2].h, ((t - 4) // 2 + 1) * 16)
                m = vector.scalar_tensor_tensor(out=ostage[:, sl4:sl4 + D],
                                                in0=oin[:, sl2:sl2 + D],
                                                scalar=mask_sb[:, t:t + 1],
                                                in1=x2in[:, sl2:sl2 + D],
                                                op0=ALU.mult, op1=ALU.add)
                s_fe.inc(m)

    ep_es.close()
    ffn_es.close()
    return finish()


_NC_CACHE = {}


def _get_nc(stop_after=None, taps=()):
    key = (stop_after, tuple(sorted(taps)))
    if key not in _NC_CACHE:
        _NC_CACHE[key] = build_nc(stop_after, taps)
    return _NC_CACHE[key]


def _split_hi_lo(r64):
    hi = np.clip(r64.astype(np.float32), -PCLIP, PCLIP)
    lo = (r64 - hi.astype(np.float64)).astype(np.float32)
    return hi, lo


def make_angle_tables():
    i = np.arange(HD // 2, dtype=np.float64)
    fraction = (2.0 * i) / HD
    timescale = 10000.0 ** fraction
    pos = np.arange(S, dtype=np.float64)
    ang = pos[None, :] / timescale[:, None]
    ang2 = np.repeat(ang, 2, axis=0)
    sign = np.where(np.arange(HD) % 2 == 0, -1.0, 1.0)[:, None]
    a_sin = sign * ang2
    a_cos = np.pi / 2 - ang2
    fold = lambda a: np.mod(a + np.pi, 2 * np.pi) - np.pi
    sh, sl = _split_hi_lo(fold(a_sin))
    ch_, cl = _split_hi_lo(fold(a_cos))
    return sh, sl, ch_, cl


def make_consts():
    tri = (np.arange(HD)[:, None] <= np.arange(HD)[None, :])
    import ml_dtypes
    tri = tri.astype(ml_dtypes.bfloat16)
    ident = np.eye(HD, dtype=np.float32)
    ones = np.ones((HD, 1), dtype=np.float32)
    return tri, ident, ones


def kernel(L0_attn_x, L0_attn_wq, L0_attn_wk, L0_attn_wv, L0_attn_wo,
           L0_ffn_w_r, L0_ffn_b_r, L0_ffn_w1, L0_ffn_b1, L0_ffn_w2, L0_ffn_b2,
           stop_after=None, taps=()):
    x = np.ascontiguousarray(np.asarray(L0_attn_x, dtype=np.float32))
    wq = np.asarray(L0_attn_wq, dtype=np.float32).reshape(D, NH, HD)
    wk = np.asarray(L0_attn_wk, dtype=np.float32).reshape(D, NH, HD)
    wv = np.asarray(L0_attn_wv, dtype=np.float32).reshape(D, NH, HD)
    wo = np.ascontiguousarray(np.asarray(L0_attn_wo, dtype=np.float32).reshape(NH * HD, D))
    w_r = np.asarray(L0_ffn_w_r, dtype=np.float32).reshape(1, D)
    b_r = np.asarray(L0_ffn_b_r, dtype=np.float32).reshape(1, 1)
    w1f = np.asarray(L0_ffn_w1, dtype=np.float32)
    w1 = np.ascontiguousarray(
        w1f.reshape(D // HD, HD, DF // HD, HD).transpose(2, 1, 0, 3)
        .reshape(DF, D))
    b1c = np.ascontiguousarray(
        np.asarray(L0_ffn_b1, dtype=np.float32).reshape(DF // HD, HD).T)
    w2f = np.asarray(L0_ffn_w2, dtype=np.float32)
    w2 = np.ascontiguousarray(
        w2f.reshape(DF // HD, HD, D // HD, HD).transpose(2, 1, 0, 3)
        .reshape(D, DF))
    b2c = np.ascontiguousarray(
        np.asarray(L0_ffn_b2, dtype=np.float32).reshape(D // HD, HD).T)
    sh, sl, ch_, cl = make_angle_tables()
    tri, ident, ones = make_consts()

    p = np.arange(HD, dtype=np.uint64)
    in_maps = []
    for c in range(8):
        row, ts = c // 4, c % 4
        hg = c % 4
        agidx = np.empty((HD, NH), dtype=np.uint32)
        hhl = max(1, NHL // 2)
        for i in range(NH):
            r = row * 4 + i // NHL
            l = i % NHL
            lh = l - hhl if l >= hhl else l
            nh_half = hhl if l < hhl else NHL - hhl
            r_glob = r * (nh_half * HD) + lh * HD + p
            agidx[:, i] = (r_glob * (S // TS) + ts).astype(np.uint32)
        scidx = (row * HD + p).astype(np.uint32).reshape(HD, 1)
        in_maps.append({
            "x_row": x[row],
            "x_sl": np.ascontiguousarray(x[row, ts * TS:(ts + 1) * TS]),
            "wq_g": np.ascontiguousarray(wq[:, NHL * hg:NHL * hg + NHL, :].reshape(D, NHL * HD)),
            "wk_g": np.ascontiguousarray(wk[:, NHL * hg:NHL * hg + NHL, :].reshape(D, NHL * HD)),
            "wv_g": np.ascontiguousarray(wv[:, NHL * hg:NHL * hg + NHL, :].reshape(D, NHL * HD)),
            "wo_in": wo, "wr_in": w_r, "br_in": b_r,
            "w1_in": w1, "b1c_in": b1c, "w2_in": w2, "b2c_in": b2c,
            "ash_in": sh, "asl_in": sl, "ach_in": ch_, "acl_in": cl,
            "tri_in": tri, "id_in": ident, "ones_in": ones,
            "onesr_in": np.ones((1, HD), dtype=np.float32),
            "agidx_in": agidx, "scidx_in": scidx,
        })
    nc = _get_nc(stop_after, taps)
    res = run_bass_kernel_spmd(nc, in_maps, core_ids=list(range(8)))
    if stop_after is not None or taps:
        return res
    out = np.empty((B, S, D), dtype=np.float32)
    for c in range(8):
        row, ts = c // 4, c % 4
        out[row, ts * TS:(ts + 1) * TS] = np.asarray(res.results[c]["out_sl"])
    return out


# version 23
# speedup vs baseline: 1.0187x; 1.0187x over previous
import math
from contextlib import ExitStack

import numpy as np
import concourse.bass as bass
import concourse.mybir as mybir
from concourse.bass_utils import run_bass_kernel_spmd
from concourse.bass_core_rust import IndirectOffsetOnAxis

F32 = mybir.dt.float32
BF16 = mybir.dt.bfloat16
U32 = mybir.dt.uint32
AF = mybir.ActivationFunctionType
ALU = mybir.AluOpType

B, S, D, NH, HD, DF = 2, 4096, 2048, 16, 128, 8192
N_ITER = 56
EPS = 1e-5
SWAP32 = [i ^ 1 for i in range(32)]
PCLIP = 3.1415925


def _recalc():
    global NHL, TS, CH, NCH, NKT, NTT, KSEL, SCALE, DT_, FSL, DSL, GPH, QKW
    NHL = NH // 4
    TS = S // 4
    CH = 512 if S >= 2048 else max(HD, S // 8)
    NCH = S // CH
    NKT = S // HD
    NTT = TS // HD
    KSEL = S // 2
    SCALE = 1.0 / math.sqrt(HD)
    DT_ = D // HD
    FSL = DF // HD
    DSL = D // HD
    GPH = CH // HD
    QKW = NHL * HD


_recalc()


class Sem:
    def __init__(self, nc, es, name):
        self.h = es.enter_context(nc.semaphore(name))
        self.n = 0

    def inc(self, instr, k=1):
        instr.then_inc(self.h, k)
        self.n += k
        return instr


def build_nc(stop_after=None, taps=()):
    nc = bass.Bass("TRN2", target_bir_lowering=False, debug=False)

    x_row = nc.dram_tensor("x_row", [S, D], F32, kind="ExternalInput")
    x_sl = nc.dram_tensor("x_sl", [TS, D], F32, kind="ExternalInput")
    wq_g = nc.dram_tensor("wq_g", [D, NHL * HD], F32, kind="ExternalInput")
    wk_g = nc.dram_tensor("wk_g", [D, NHL * HD], F32, kind="ExternalInput")
    wv_g = nc.dram_tensor("wv_g", [D, NHL * HD], F32, kind="ExternalInput")
    wo_in = nc.dram_tensor("wo_in", [NH * HD, D], F32, kind="ExternalInput")
    wr_in = nc.dram_tensor("wr_in", [1, D], F32, kind="ExternalInput")
    br_in = nc.dram_tensor("br_in", [1, 1], F32, kind="ExternalInput")
    w1_in = nc.dram_tensor("w1_in", [DF, D], F32, kind="ExternalInput")
    b1c_in = nc.dram_tensor("b1c_in", [HD, DF // HD], F32, kind="ExternalInput")
    w2_in = nc.dram_tensor("w2_in", [D, DF], F32, kind="ExternalInput")
    b2c_in = nc.dram_tensor("b2c_in", [HD, D // HD], F32, kind="ExternalInput")
    ash_in = nc.dram_tensor("ash_in", [HD, S], F32, kind="ExternalInput")
    asl_in = nc.dram_tensor("asl_in", [HD, S], F32, kind="ExternalInput")
    ach_in = nc.dram_tensor("ach_in", [HD, S], F32, kind="ExternalInput")
    acl_in = nc.dram_tensor("acl_in", [HD, S], F32, kind="ExternalInput")
    tri_in = nc.dram_tensor("tri_in", [HD, HD], BF16, kind="ExternalInput")
    id_in = nc.dram_tensor("id_in", [HD, HD], F32, kind="ExternalInput")
    ones_in = nc.dram_tensor("ones_in", [HD, 1], F32, kind="ExternalInput")
    onesr_in = nc.dram_tensor("onesr_in", [1, HD], F32, kind="ExternalInput")
    agidx_in = nc.dram_tensor("agidx_in", [HD, NH], U32, kind="ExternalInput")
    scidx_in = nc.dram_tensor("scidx_in", [HD, 1], U32, kind="ExternalInput")

    out_sl = nc.dram_tensor("out_sl", [TS, D], F32, kind="ExternalOutput")
    tap_t = {}
    for name, shape, dt in (
        ("xln", [S, D], BF16), ("qT", [HD, NHL * S], BF16),
        ("kT", [HD, NHL * S], BF16), ("v", [HD, NKT * QKW], BF16),
        ("attnT", [NHL * HD, S], BF16), ("x2", [TS, D], F32),
        ("x2ln", [TS, D], BF16), ("sc", [1, TS], F32), ("thr", [1, 4], F32),
        ("atm", [HD, NH * TS], BF16), ("scg", [HD, S // HD], F32),
        ("hT", [HD, min(16, FSL) * TS], BF16), ("o2", [TS, D], F32),
    ):
        if name in taps:
            tap_t[name] = nc.dram_tensor("tap_" + name, shape, dt, kind="ExternalOutput")

    xln_hbm = nc.dram_tensor("xln_hbm", [S, D], BF16, kind="Internal")
    q_hbm = nc.dram_tensor("q_hbm", [HD, NHL * S], BF16, kind="Internal")
    x2_hbm = nc.dram_tensor("x2_hbm", [TS, D], F32, kind="Internal")
    x2ln_hbm = nc.dram_tensor("x2ln_hbm", [TS, D], BF16, kind="Internal")
    o2_hbm = nc.dram_tensor("o2_hbm", [D, TS], BF16, kind="Internal")
    HHL0 = max(1, (NH // 4) // 2)
    agin1a = nc.dram_tensor("agin1a", [HHL0 * HD, S], BF16, kind="Internal")
    agin1b = nc.dram_tensor("agin1b", [((NH // 4) - HHL0) * HD, S], BF16, kind="Internal")
    HHL = HHL0
    agout1a = nc.dram_tensor("agout1a", [8 * HHL * HD * (S // TS), TS], BF16, kind="Internal")
    agout1b = nc.dram_tensor("agout1b", [8 * (NHL - HHL) * HD * (S // TS), TS], BF16, kind="Internal")
    agin2 = nc.dram_tensor("agin2", [1, TS], F32, kind="Internal")
    agout2 = nc.dram_tensor("agout2", [8 * TS * HD // S, S // HD], F32, kind="Internal")

    es = ExitStack()

    def SB(stack, name, shape, dtype):
        return stack.enter_context(nc.sbuf_tensor(name, shape, dtype))

    def PS(stack, name, shape, dtype=F32):
        return stack.enter_context(nc.psum_tensor(name, shape, dtype))

    ones_row = SB(es, "ones_row", [1, HD], F32)
    ones_colb = SB(es, "ones_colb", [HD, 1], BF16)
    ones_colf = SB(es, "ones_colf", [HD, 1], F32)
    tri_sb = SB(es, "tri_sb", [HD, HD], BF16)
    id_sb = SB(es, "id_sb", [HD, HD], F32)
    wr_sb = SB(es, "wr_sb", [1, D], F32)
    br_sb = SB(es, "br_sb", [1, 1], F32)
    b1_sb = SB(es, "b1_sb", [HD, DF // HD], F32)
    b2c_sb = SB(es, "b2c_sb", [HD, D // HD], F32)
    agidx = SB(es, "agidx", [HD, NH], U32)
    scidx = SB(es, "scidx", [HD, 1], U32)
    sc_mine = SB(es, "sc_mine", [HD, NTT], F32)
    mask_sb = SB(es, "mask_sb", [HD, NTT], F32)
    small = SB(es, "small", [HD, 32], F32)
    tiny = SB(es, "tiny", [1, 32], F32)
    tiny_i = SB(es, "tiny_i", [1, 4], mybir.dt.int32)
    junkb = SB(es, "junkb", [HD, D], BF16)
    junkb2 = SB(es, "junkb2", [HD, D], BF16)

    sems = []

    def mk(name):
        s = Sem(nc, es, name)
        sems.append(s)
        return s

    s_cst = mk("s_cst"); s_x = mk("s_x"); s_st = mk("s_st"); s_sq = mk("s_sq")
    s_ln = mk("s_ln"); s_xw = mk("s_xw"); s_xt = mk("s_xt"); s_ang = mk("s_ang")
    s_xtp = [mk("s_xtp0"), mk("s_xtp1")]
    s_tab = mk("s_tab"); s_qk = mk("s_qk"); s_rope = mk("s_rope"); s_vev = mk("s_vev")
    s_vf = mk("s_vf"); s_sf = mk("s_sf")
    s_lg = mk("s_lg"); s_ex = mk("s_ex"); s_mk = mk("s_mk"); s_pv = mk("s_pv")
    s_rc = mk("s_rc"); s_rb = mk("s_rb"); s_ev = mk("s_ev"); s_ed = mk("s_ed")
    s_ag1 = mk("s_ag1"); s_am = mk("s_am"); s_wo = mk("s_wo"); s_op = mk("s_op")
    s_xs = mk("s_xs"); s_x2 = mk("s_x2"); s_sd = mk("s_sd"); s_ag2 = mk("s_ag2")
    s_sg = mk("s_sg"); s_bc = mk("s_bc"); s_ct = mk("s_ct"); s_it = mk("s_it"); s_ms = mk("s_ms")
    s_xt2 = mk("s_xt2"); s_w1 = mk("s_w1"); s_h = mk("s_h")
    s_g = mk("s_g"); s_w2 = mk("s_w2"); s_o2 = mk("s_o2"); s_tb = mk("s_tb")
    s_te = mk("s_te"); s_o2w = mk("s_o2w"); s_fi = mk("s_fi"); s_fe = mk("s_fe")
    s_out = mk("s_out"); s_qg = mk("s_qg"); s_cg = mk("s_cg"); s_qw = mk("s_qw")
    s_xp = [mk("s_xp0"), mk("s_xp1")]
    s_xwp = [mk("s_xwp0"), mk("s_xwp1")]
    s_qwp = [mk(f"s_qwp{i}") for i in range(4)]
    s_edp = [mk("s_edp0"), mk("s_edp1")]
    s_xsp = [mk("s_xsp0"), mk("s_xsp1")]
    s_w1p = [mk("s_w1p0"), mk("s_w1p1")]
    s_w2p = [mk("s_w2p0"), mk("s_w2p1")]
    s_owp = [mk("s_owp0"), mk("s_owp1")]
    s_fip = [mk("s_fip0"), mk("s_fip1")]
    s_oup = [mk("s_oup0"), mk("s_oup1")]
    s_wsp = [mk(f"s_wsp{i}") for i in range(4)]
    s_wop = [mk("s_wop0"), mk("s_wop1")]
    s_saa = mk("s_saa"); s_sta = mk("s_sta")
    s_qgp = [mk("s_qgp0"), mk("s_qgp1")]

    _block_cm = nc.Block()
    _the_blk = _block_cm.__enter__()

    class _Shim:
        @property
        def sync(self):
            return _the_blk.sync

        @property
        def gpsimd(self):
            return _the_blk.gpsimd

        @property
        def scalar(self):
            return _the_blk.scalar

        @property
        def vector(self):
            return _the_blk.vector

        @property
        def tensor(self):
            return _the_blk.tensor

    _blk_shim = _Shim()

    def vfl(vector):
        s_vf.n += 1
        vector.engine_nop().then_inc(s_vf.h, 1)
        vector.wait_ge(s_vf.h, s_vf.n)



    def finish(extra_waits=()):
        @_blk_shim.sync
        def _(sync):
            for sem, val in extra_waits:
                sync.wait_ge(sem.h, val)
            sync.wait_ge(s_out.h, s_out.n)
            sync.wait_ge(s_oup[0].h, s_oup[0].n)
            sync.wait_ge(s_oup[1].h, s_oup[1].n)
        _block_cm.__exit__(None, None, None)
        nc.all_engine_barrier()
        nc.clear_and_free_semaphores([s.h for s in sems])
        nc.all_engine_barrier()
        es.close()
        return nc

    att_es = ExitStack()
    kTb = SB(att_es, "kTb", [HD, NHL * S], BF16)
    vsb = SB(att_es, "vsb", [HD, NKT * QKW], BF16)

    if True:
        blk = _blk_shim
        @blk.sync
        def _(sync):
            for dst, src in ((ones_row[:], onesr_in[:]),
                             (ones_colf[:], ones_in[:]),
                             (tri_sb[:], tri_in[:]), (id_sb[:], id_in[:]),
                             (wr_sb[:], wr_in[:]), (br_sb[:], br_in[:]),
                             (b1_sb[:], b1c_in[:]), (b2c_sb[:], b2c_in[:]),
                             (agidx[:], agidx_in[:]), (scidx[:], scidx_in[:])):
                s_cst.inc(sync.dma_start(out=dst, in_=src), 16)

        @blk.gpsimd
        def _(gpsimd):
            s_cst.inc(gpsimd.memset(ones_colb[:], 1.0), 1)
    CST_ALL = s_cst.n

    b_es = ExitStack()
    wqb = SB(b_es, "wqb", [HD, DT_ * QKW], BF16)
    wkb = SB(b_es, "wkb", [HD, DT_ * QKW], BF16)
    wvb = SB(b_es, "wvb", [HD, DT_ * QKW], BF16)
    xt_f32 = SB(b_es, "xt_f32", [HD, 2 * D], F32)
    xln_t = SB(b_es, "xln_t", [HD, 2 * D], BF16)
    xlt = SB(b_es, "xlt", [HD, 2 * DT_ * CH], BF16)
    tab_sin = SB(b_es, "tab_sin", [HD, CH], F32)
    tab_cos = SB(b_es, "tab_cos", [HD, CH], F32)
    ang_buf = SB(b_es, "ang_buf", [HD, 2 * CH], F32)
    rtmp_s = SB(b_es, "rtmp_s", [HD, 2 * CH], F32)
    rtmp_d = SB(b_es, "rtmp_d", [HD, max(2 * CH, 2 * QKW)], F32)
    def rtmp_a(sl=slice(None)):
        return rtmp_d[:, CH:2 * CH]
    def rtmp_b(sl=slice(None)):
        return rtmp_s[:, CH:2 * CH]
    qstage = SB(b_es, "qstage", [HD, NHL * CH], BF16)
    qk_ps = [PS(b_es, f"qk_ps{i}", [HD, CH]) for i in range(4)]
    v_ps = [PS(b_es, f"v_ps{i}", [HD, QKW]) for i in range(2)]

    _wslots_s = max(1, (2 * CH) // QKW)
    _wslots_d = max(1, (2 * CH) // QKW)
    NWS = min(4, _wslots_s + _wslots_d)

    def wslot(k):
        i = k % NWS
        if i < _wslots_s:
            return rtmp_s[:, i * QKW:(i + 1) * QKW]
        i -= _wslots_s
        return rtmp_d[:, i * QKW:(i + 1) * QKW]

    tpc = CH // HD
    stc = [0] * NCH

    def qkc(c, li):
        return c * (2 * NHL + GPH) + li + 1

    def qvc(c, t):
        return c * (2 * NHL + GPH) + 2 * NHL + t + 1

    def emit_chunk_transposes(sync, t):
        if (t + 1) % tpc != 0:
            return
        c = (t + 1) // tpc - 1
        cs = c * CH
        sync.wait_ge(s_xwp[t % 2].h, (t // 2 + 1) * 16)
        if t >= 1:
            sync.wait_ge(s_xwp[(t - 1) % 2].h, ((t - 1) // 2 + 1) * 16)
        if c >= 2:
            sync.wait_ge(s_qk.h, (2 * NHL + GPH) * (c - 1))
        xb = (c % 2) * DT_ * CH
        for dt_ in range(DT_):
            s_xtp[c % 2].inc(sync.dma_start(
                out=xlt[:, xb + dt_ * CH:xb + (dt_ + 1) * CH],
                in_=xln_hbm[cs:cs + CH, dt_ * HD:(dt_ + 1) * HD],
                transpose=True), 16)

    if True:
        blk = _blk_shim
        @blk.sync
        def _(sync):
            for g in range(NKT):
                if g >= 2:
                    sync.wait_ge(s_ln.h, g - 1)
                    s_xwp[g % 2].inc(sync.dma_start(
                        out=xln_hbm[(g - 2) * HD:(g - 1) * HD, :],
                        in_=xln_t[:, (g % 2) * D:(g % 2 + 1) * D]), 16)
                    emit_chunk_transposes(sync, g - 2)
                s_xp[g % 2].inc(sync.dma_start(
                    out=xt_f32[:, (g % 2) * D:(g % 2 + 1) * D],
                    in_=x_row[g * HD:(g + 1) * HD, :]), 16)
            for g in (NKT - 2, NKT - 1):
                sync.wait_ge(s_ln.h, g + 1)
                s_xwp[g % 2].inc(sync.dma_start(
                    out=xln_hbm[g * HD:(g + 1) * HD, :],
                    in_=xln_t[:, (g % 2) * D:(g % 2 + 1) * D]), 16)
                emit_chunk_transposes(sync, g)

        @blk.gpsimd
        def _(gpsimd):
            for c in range(NCH):
                cs = c * CH
                if c >= 1:
                    gpsimd.wait_ge(s_sta.h, c)
                for i, src in enumerate((ash_in, asl_in)):
                    s_ang.inc(gpsimd.dma_start(
                        out=ang_buf[:, i * CH:(i + 1) * CH],
                        in_=src[:, cs:cs + CH]), 16)
                gpsimd.wait_ge(s_saa.h, c + 1)
                for i, src in enumerate((ach_in, acl_in)):
                    s_ang.inc(gpsimd.dma_start(
                        out=ang_buf[:, i * CH:(i + 1) * CH],
                        in_=src[:, cs:cs + CH]), 16)

    def ln_tile_v(vector, g):
        lo, hi = w_batch(g)
        if g == 0:
            lo = 0
        for k in range(lo, hi):
            w_cast(vector, k)
        p = g % 4
        vector.wait_ge(s_x.h, 2 * (g + 1))
        vector.tensor_scalar(out=small[:, 16 + p:17 + p], in0=small[:, 8 + p:9 + p],
                             scalar1=1.0 / D, scalar2=None, op0=ALU.mult)
        vector.tensor_scalar(out=small[:, 20 + p:21 + p], in0=small[:, 12 + p:13 + p],
                             scalar1=1.0 / D, scalar2=EPS, op0=ALU.mult,
                             op1=ALU.add)
        vfl(vector)
        vector.tensor_tensor(out=small[:, 28 + p:29 + p],
                             in0=small[:, 16 + p:17 + p],
                             in1=small[:, 16 + p:17 + p], op=ALU.mult)
        vfl(vector)
        vector.tensor_tensor(out=small[:, 20 + p:21 + p],
                             in0=small[:, 20 + p:21 + p],
                             in1=small[:, 28 + p:29 + p], op=ALU.subtract)
        s_xw.inc(vector.engine_nop())
        vector.wait_ge(s_sq.h, g + 1)
        vector.reciprocal(out=small[:, 24 + p:25 + p], in_=small[:, 28 + p:29 + p])
        vfl(vector)
        vector.scalar_tensor_tensor(out=small[:, 0 + p:1 + p],
                                    in0=small[:, 16 + p:17 + p], scalar=-1.0,
                                    in1=small[:, 24 + p:25 + p],
                                    op0=ALU.mult, op1=ALU.mult)
        vfl(vector)
        s_xt.inc(vector.engine_nop())

    def ln_tile_s(scalar, g):
        p = g % 4
        xv = xt_f32[:, (g % 2) * D:(g % 2 + 1) * D]
        lo, hi = w_batch(g)
        for k in range(lo, hi):
            w_ld(scalar, k)
        scalar.wait_ge(s_xp[g % 2].h, (g // 2 + 1) * 16)
        scalar.activation(out=junkb[:], in_=xv, func=AF.Identity,
                          accum_out=small[:, 8 + p:9 + p])
        a = scalar.activation(out=junkb2[:], in_=xv, func=AF.Square,
                              accum_out=small[:, 12 + p:13 + p])
        s_x.inc(a)
        scalar.wait_ge(s_x.h, s_x.n)
        a = scalar.activation(out=junkb2[:, 0:1], in_=small[:, 8 + p:9 + p],
                              func=AF.Copy)
        s_x.inc(a)
        scalar.wait_ge(s_xw.h, g + 1)
        s_sq.inc(scalar.activation(out=small[:, 28 + p:29 + p],
                                   in_=small[:, 20 + p:21 + p], func=AF.Sqrt))
        scalar.wait_ge(s_xt.h, g + 1)
        if g >= 2:
            scalar.wait_ge(s_xwp[g % 2].h, (g // 2) * 16)
        a = scalar.activation(out=xln_t[:, (g % 2) * D:(g % 2 + 1) * D], in_=xv,
                              func=AF.Identity, bias=small[:, 0 + p:1 + p],
                              scale=small[:, 24 + p:25 + p])
        s_ln.inc(a)

    def emit_ln_chunk_v(c):
        @_blk_shim.vector
        def _(vector, c=c):
            for g in range(c * tpc, (c + 1) * tpc):
                ln_tile_v(vector, g)

    def emit_sq_chunk_s(c):
        @_blk_shim.scalar
        def _(scalar, c=c):
            for g in range(c * tpc, (c + 1) * tpc):
                ln_tile_s(scalar, g)

    wsrcs = []
    for _wd, _ws in ((wqb, wq_g), (wkb, wk_g), (wvb, wv_g)):
        for _dt in range(DT_):
            wsrcs.append((_wd, _ws, _dt))
    NW_ = len(wsrcs)

    def w_ld(scalar, k):
        if k >= NWS:
            scalar.wait_ge(s_cg.h, k - NWS + 1)
        _, w_src, dt_ = wsrcs[k]
        s_wsp[k % NWS].inc(scalar.dma_start(
            out=wslot(k),
            in_=w_src[dt_ * HD:(dt_ + 1) * HD, :]), 16)

    if True:
        blk = _blk_shim
        @blk.scalar
        def _(scalar):
            for k in range(min(NWS, NW_)):
                w_ld(scalar, k)

    def w_cast(vector, k):
        w_dst, _, dt_ = wsrcs[k]
        vector.wait_ge(s_wsp[k % NWS].h, (k // NWS + 1) * 16)
        m = vector.tensor_scalar(
            out=w_dst[:, dt_ * QKW:(dt_ + 1) * QKW],
            in0=wslot(k),
            scalar1=1.0, scalar2=None, op0=ALU.mult)
        s_cg.inc(m)

    _WB = max(1, -(-(NW_ - min(NWS, NW_)) // max(1, 2 * tpc)))

    def w_batch(g):
        lo = min(NW_, min(NWS, NW_) + g * _WB)
        hi = min(NW_, min(NWS, NW_) + (g + 1) * _WB)
        return lo, hi

    npre = min(2, NCH)
    emit_ln_chunk_v(0)
    if NCH > 1:
        emit_ln_chunk_v(1)

    CG_W = s_cg.n
    emit_sq_chunk_s(0)
    if NCH > 1:
        emit_sq_chunk_s(1)

    for c in range(NCH):
        cs = c * CH
        if True:
            blk = _blk_shim
            @blk.vector
            def _(vector, c=c):
                if c >= 1:
                    vector.wait_ge(s_tab.h, c)
                    vfl(vector)
                else:
                    vector.wait_ge(s_cg.h, CG_W)
                vector.wait_ge(s_ang.h, c * 64 + 32)
                m = vector.tensor_tensor(out=rtmp_a(), in0=ang_buf[:, 0:CH],
                                         in1=ang_buf[:, CH:2 * CH], op=ALU.add)
                s_saa.inc(m)
                vector.wait_ge(s_ang.h, (c + 1) * 64)
                m = vector.tensor_tensor(out=rtmp_b(), in0=ang_buf[:, 0:CH],
                                         in1=ang_buf[:, CH:2 * CH], op=ALU.add)
                s_sta.inc(m)
                vfl(vector)
                vector.tensor_scalar(out=rtmp_a(), in0=rtmp_a(), scalar1=PCLIP,
                                     scalar2=-PCLIP, op0=ALU.min, op1=ALU.max)
                vector.tensor_scalar(out=rtmp_b(), in0=rtmp_b(), scalar1=PCLIP,
                                     scalar2=-PCLIP, op0=ALU.min, op1=ALU.max)
                vfl(vector)
                s_st.inc(vector.engine_nop())
                stc[c] = s_st.n

            @blk.scalar
            def _(scalar, c=c):
                scalar.wait_ge(s_st.h, stc[c])
                scalar.activation(out=tab_sin[:], in_=rtmp_a(), func=AF.Sin)
                s_tab.inc(scalar.activation(out=tab_cos[:], in_=rtmp_b(), func=AF.Sin))
                for n in range(NHL):
                    gi = c * (2 * NHL) + n * 2
                    scalar.wait_ge(s_rope.h, gi + 1)
                    s_qwp[n].inc(scalar.dma_start(
                        out=q_hbm[:, n * S + cs:n * S + cs + CH],
                        in_=qstage[:, n * CH:(n + 1) * CH]), 16)

            @blk.tensor
            def _(tensor, c=c, cs=cs):
                tensor.wait_ge(s_cg.h, CG_W)
                tensor.wait_ge(s_xtp[c % 2].h, (c // 2 + 1) * DT_ * 16)
                xb = (c % 2) * DT_ * CH
                for li in range(2 * NHL):
                    n, isk = li // 2, li % 2
                    gi = c * (2 * NHL) + li
                    pi = gi % 4
                    if gi >= 4:
                        tensor.wait_ge(s_rope.h, gi - 3)
                    wsb = wkb if isk else wqb
                    for dt_ in range(DT_):
                        m = tensor.matmul(qk_ps[pi][:],
                                          wsb[:, dt_ * QKW + n * HD:dt_ * QKW + (n + 1) * HD],
                                          xlt[:, xb + dt_ * CH:xb + (dt_ + 1) * CH],
                                          start=(dt_ == 0), stop=(dt_ == DT_ - 1))
                    s_qk.inc(m)
                for t in range(GPH):
                    vi = c * GPH + t
                    if vi >= 2:
                        tensor.wait_ge(s_vev.h, vi - 1)
                    for dt_ in range(DT_):
                        m = tensor.matmul(v_ps[vi % 2][:],
                                          xlt[:, xb + dt_ * CH + t * HD:xb + dt_ * CH + (t + 1) * HD],
                                          wvb[:, dt_ * QKW:(dt_ + 1) * QKW],
                                          start=(dt_ == 0), stop=(dt_ == DT_ - 1))
                    s_qk.inc(m)

            @blk.vector
            def _(vector, c=c, cs=cs):
                vector.wait_ge(s_tab.h, c + 1)
                for li in range(2 * NHL):
                    n, isk = li // 2, li % 2
                    gi = c * (2 * NHL) + li
                    pi = gi % 4
                    vector.wait_ge(s_qk.h, qkc(c, li))
                    if not isk and c >= 1:
                        vector.wait_ge(s_qwp[n].h, c * 16)
                    p = qk_ps[pi]
                    pr = (gi % 2) * CH
                    vector.stream_shuffle(out=rtmp_s[:, pr:pr + CH], in_=p[:], mask=SWAP32)
                    vector.tensor_tensor(out=rtmp_d[:, pr:pr + CH], in0=p[:],
                                         in1=tab_cos[:], op=ALU.mult)
                    vfl(vector)
                    vector.tensor_tensor(out=rtmp_s[:, pr:pr + CH],
                                         in0=rtmp_s[:, pr:pr + CH], in1=tab_sin[:],
                                         op=ALU.mult)
                    vfl(vector)
                    if isk:
                        vector.tensor_tensor(out=kTb[:, n * S + cs:n * S + cs + CH],
                                             in0=rtmp_d[:, pr:pr + CH],
                                             in1=rtmp_s[:, pr:pr + CH], op=ALU.add)
                    else:
                        vector.tensor_tensor(out=qstage[:, n * CH:(n + 1) * CH],
                                             in0=rtmp_d[:, pr:pr + CH],
                                             in1=rtmp_s[:, pr:pr + CH], op=ALU.add)
                    s_rope.inc(vector.engine_nop())
                for t in range(GPH):
                    vi = c * GPH + t
                    vector.wait_ge(s_qk.h, qvc(c, t))
                    vector.tensor_copy(out=vsb[:, vi * QKW:(vi + 1) * QKW],
                                       in_=v_ps[vi % 2][:])
                    s_vev.inc(vector.engine_nop())

        if c + 2 < NCH:
            emit_ln_chunk_v(c + 2)
            emit_sq_chunk_s(c + 2)

    LN_ALL = s_ln.n
    if "xln" in tap_t:
        if True:
            blk = _blk_shim
            @blk.sync
            def _(sync):
                sync.wait_ge(s_xwp[0].h, s_xwp[0].n)
                sync.wait_ge(s_xwp[1].h, s_xwp[1].n)
                s_out.inc(sync.dma_start(out=tap_t["xln"][:], in_=xln_hbm[:]), 16)

    ROPE_ALL = s_rope.n
    VEV_ALL = s_vev.n
    QK_ALL = s_qk.n
    if "qT" in tap_t or "kT" in tap_t or "v" in tap_t:
        if True:
            blk = _blk_shim
            @blk.sync
            def _(sync):
                sync.wait_ge(s_rope.h, ROPE_ALL)
                sync.wait_ge(s_vev.h, VEV_ALL)
                if "qT" in tap_t:
                    for _s in s_qwp:
                        sync.wait_ge(_s.h, _s.n)
                    s_out.inc(sync.dma_start(out=tap_t["qT"][:], in_=q_hbm[:]), 16)
                if "kT" in tap_t:
                    s_out.inc(sync.dma_start(out=tap_t["kT"][:], in_=kTb[:]), 16)
                if "v" in tap_t:
                    s_out.inc(sync.dma_start(out=tap_t["v"][:], in_=vsb[:]), 16)

    b_es.close()
    if stop_after == "qkv":
        att_es.close()
        return finish()

    p1_es = ExitStack()
    qg = SB(p1_es, "qg", [HD, 2 * CH], BF16)
    prob = SB(p1_es, "prob", [HD, 4 * CH], BF16)
    recip_sb = SB(p1_es, "recip_sb", [1, CH], F32)
    recipb_sb = SB(p1_es, "recipb_sb", [HD, CH], F32)
    astage = SB(p1_es, "astage", [HD, 2 * CH], BF16)
    att_ps = [PS(p1_es, f"att_ps{i}", [HD, CH]) for i in range(2)]
    lg_ps = [PS(p1_es, f"lg_ps{i}", [HD, CH]) for i in range(2)]
    rs_ps = [PS(p1_es, f"rs_ps{i}", [1, CH]) for i in range(2)]
    rb_ps = PS(p1_es, "rb_ps", [HD, CH])

    def _nk(gi):
        return GPH * (gi % NCH + 1)

    def _pre(gi):
        g = gi % NCH
        return (gi // NCH) * (GPH * NCH * (NCH + 1) // 2) + GPH * g * (g + 1) // 2

    def lgc(gi, j):
        return _pre(gi) + j + 1
    pvc = lgc

    _pex = [0]
    for _gi in range(NHL * NCH):
        _pex.append(_pex[-1] + (_nk(_gi) + 1) // 2)

    def PEX(gi):
        return _pex[gi]

    def exc2(gi, j):
        return PEX(gi) + j // 2 + 1

    def mkc(gi, j):
        return gi * GPH + (j - GPH * (gi % NCH)) + 1

    seq = [(n, g) for n in range(NHL) for g in range(NCH)]

    def qg_load(sync, gi):
        n, g = seq[gi]
        gq0 = g * CH
        sync.wait_ge(s_qwp[n].h, (g + 1) * 16)
        if gi <= 1:
            sync.wait_ge(s_qk.h, QK_ALL)
            sync.wait_ge(s_rope.h, ROPE_ALL)
            sync.wait_ge(s_vev.h, VEV_ALL)
        if gi >= 2:
            sync.wait_ge(s_lg.h, _pre(gi - 1))
        s_qgp[gi % 2].inc(sync.dma_start(out=qg[:, (gi % 2) * CH:(gi % 2 + 1) * CH],
                                         in_=q_hbm[:, n * S + gq0:n * S + gq0 + CH]), 16)

    def rb_bcast(tensor, gi):
        tensor.wait_ge(s_rc.h, gi + 1)
        if gi >= 1:
            tensor.wait_ge(s_ev.h, gi)
        m = tensor.matmul(rb_ps[:], ones_row[:], recip_sb[:],
                          start=True, stop=True)
        s_rb.inc(m)
    if True:
        blk = _blk_shim
        @blk.sync
        def _(sync):
            qg_load(sync, 0)

    for gi, (n, g) in enumerate(seq):
        nk = GPH * (g + 1)
        gq0 = g * CH
        if True:
            blk = _blk_shim
            @blk.tensor
            def _(tensor, n=n, g=g, gi=gi, nk=nk, gq0=gq0):
                if gi == 0:
                    tensor.wait_ge(s_rope.h, ROPE_ALL)
                    tensor.wait_ge(s_vev.h, VEV_ALL)
                    tensor.wait_ge(s_cst.h, CST_ALL)
                tensor.wait_ge(s_qgp[gi % 2].h, (gi // 2 + 1) * 16)

                def pv_j(j):
                    js = max(0, j - GPH * g) * HD
                    diag = j >= GPH * g
                    if diag:
                        tensor.wait_ge(s_mk.h, mkc(gi, j))
                    else:
                        tensor.wait_ge(s_ex.h, exc2(gi, j))
                    pslot = (j % 4) * CH
                    tensor.matmul(att_ps[gi % 2][:, js:CH],
                                  vsb[:, j * QKW + n * HD:j * QKW + (n + 1) * HD],
                                  prob[:, pslot + js:pslot + CH],
                                  start=(j == 0), stop=(j == nk - 1))
                    m = tensor.matmul(rs_ps[gi % 2][:, js:CH], ones_colb[:],
                                      prob[:, pslot + js:pslot + CH],
                                      start=(j == 0), stop=(j == nk - 1))
                    s_pv.inc(m)

                jpairs = [list(range(nk))[i:i + 2] for i in range(0, nk, 2)]
                for pp, pr in enumerate(jpairs):
                    if pp >= 1:
                        tensor.wait_ge(s_ex.h, PEX(gi) + pp)
                    elif gi >= 1:
                        tensor.wait_ge(s_ex.h, PEX(gi))
                    for j in pr:
                        js = max(0, j - GPH * g) * HD
                        m = tensor.matmul(lg_ps[j % 2][:, js:CH],
                                          kTb[:, n * S + j * HD:n * S + (j + 1) * HD],
                                          qg[:, (gi % 2) * CH + js:(gi % 2) * CH + CH],
                                          start=True, stop=True)
                        s_lg.inc(m)
                    if pp == 0:
                        if gi >= 2:
                            tensor.wait_ge(s_ev.h, gi - 1)
                        if gi >= 1:
                            rb_bcast(tensor, gi - 1)
                    if pp >= 1:
                        for j in jpairs[pp - 1]:
                            pv_j(j)
                for j in jpairs[-1]:
                    pv_j(j)
                if gi == len(seq) - 1:
                    rb_bcast(tensor, gi)

            @blk.scalar
            def _(scalar, n=n, g=g, gi=gi, nk=nk):
                jpairs = [list(range(nk))[i:i + 2] for i in range(0, nk, 2)]
                for pp, pr in enumerate(jpairs):
                    if pp >= 2:
                        scalar.wait_ge(s_pv.h, pvc(gi, pr[-1] - 4))
                    elif gi >= 1 and pp == 0:
                        scalar.wait_ge(s_pv.h, _pre(gi))
                    scalar.wait_ge(s_lg.h, lgc(gi, pr[-1]))
                    for j in pr:
                        js = max(0, j - GPH * g) * HD
                        pslot = (j % 4) * CH
                        a = scalar.activation(out=prob[:, pslot + js:pslot + CH],
                                              in_=lg_ps[j % 2][:, js:CH],
                                              func=AF.Exp, scale=SCALE)
                    s_ex.inc(a)

            @blk.vector
            def _(vector, n=n, g=g, gi=gi, nk=nk):
                for j in range(GPH * g, nk):
                    js = (j - GPH * g) * HD
                    pslot = (j % 4) * CH
                    vector.wait_ge(s_ex.h, exc2(gi, j))
                    m = vector.tensor_tensor(out=prob[:, pslot + js:pslot + js + HD],
                                             in0=prob[:, pslot + js:pslot + js + HD],
                                             in1=tri_sb[:], op=ALU.mult)
                    s_mk.inc(m)
                vector.wait_ge(s_pv.h, pvc(gi, nk - 1))
                vector.reciprocal(out=recip_sb[:], in_=rs_ps[gi % 2][:])
                vfl(vector)
                s_rc.inc(vector.engine_nop())
                vector.wait_ge(s_rb.h, gi + 1)
                if gi >= 2:
                    vector.wait_ge(s_edp[gi % 2].h, (gi // 2) * 16)
                vector.tensor_copy(out=recipb_sb[:], in_=rb_ps[:])
                vfl(vector)
                m = vector.tensor_tensor(out=astage[:, (gi % 2) * CH:(gi % 2 + 1) * CH],
                                         in0=att_ps[gi % 2][:], in1=recipb_sb[:],
                                         op=ALU.mult)
                s_ev.inc(m)

            @blk.sync
            def _(sync, n=n, g=g, gi=gi, gq0=gq0):
                if gi + 1 < len(seq):
                    qg_load(sync, gi + 1)
                sync.wait_ge(s_ev.h, gi + 1)
                ag_t = agin1a if n < HHL else agin1b
                nn = n if n < HHL else n - HHL
                s_edp[gi % 2].inc(sync.dma_start(
                    out=ag_t[nn * HD:(nn + 1) * HD, gq0:gq0 + CH],
                    in_=astage[:, (gi % 2) * CH:(gi % 2 + 1) * CH]), 16)

    if "attnT" in tap_t:
        if True:
            blk = _blk_shim
            @blk.sync
            def _(sync):
                sync.wait_ge(s_edp[0].h, s_edp[0].n)
                sync.wait_ge(s_edp[1].h, s_edp[1].n)
                s_out.inc(sync.dma_start(out=tap_t["attnT"][:], in_=agin1a[:]), 16)

    if True:
        blk = _blk_shim
        @blk.gpsimd
        def _(gpsimd):
            nga = HHL * NCH
            gpsimd.wait_ge(s_edp[0].h, (nga - nga // 2) * 16)
            gpsimd.wait_ge(s_edp[1].h, (nga // 2) * 16)
            s_ag1.inc(gpsimd.collective_compute(
                "AllGather", ALU.bypass,
                replica_groups=[[0, 1, 2, 3, 4, 5, 6, 7]],
                ins=[agin1a.ap().opt()], outs=[agout1a.ap().opt()]), 1)
            gpsimd.wait_ge(s_edp[0].h, s_edp[0].n)
            gpsimd.wait_ge(s_edp[1].h, s_edp[1].n)
            s_ag1.inc(gpsimd.collective_compute(
                "AllGather", ALU.bypass,
                replica_groups=[[0, 1, 2, 3, 4, 5, 6, 7]],
                ins=[agin1b.ap().opt()], outs=[agout1b.ap().opt()]), 1)

    EV_ALL = s_ev.n
    ED0_ALL = s_edp[0].n
    ED1_ALL = s_edp[1].n
    p1_es.close()
    att_es.close()
    if stop_after == "attn":
        return finish(extra_waits=[(s_ag1, 2)])

    p3_es = ExitStack()
    wob = SB(p3_es, "wob", [HD, NH * D], BF16)
    wostage = SB(p3_es, "wostage", [HD, 2 * D], F32)
    atm = SB(p3_es, "atm", [HD, NH * TS], BF16)
    wrb = SB(p3_es, "wrb", [HD, D], F32)
    scr_f32 = SB(p3_es, "scr_f32", [HD, D], F32)
    xs_t = SB(p3_es, "xs_t", [HD, 2 * D], F32)
    x2st = SB(p3_es, "x2st", [HD, 2 * D], F32)
    x2lst = SB(p3_es, "x2lst", [HD, 2 * D], BF16)
    junkb3 = SB(p3_es, "junkb3", [HD, D], BF16)
    op_ps = [[PS(p3_es, f"op_ps{i}_{j}", [HD, CH]) for j in range(D // CH)]
             for i in range(2)]

    if True:
        blk = _blk_shim
        @blk.sync
        def _(sync):
            for nt in range(NH):
                if nt == 0:
                    sync.wait_ge(s_ev.h, EV_ALL)
                    sync.wait_ge(s_edp[0].h, ED0_ALL)
                    sync.wait_ge(s_edp[1].h, ED1_ALL)
                elif nt >= 2:
                    sync.wait_ge(s_wo.h, nt - 1)
                s_wop[nt % 2].inc(sync.dma_start(
                    out=wostage[:, (nt % 2) * D:(nt % 2 + 1) * D],
                    in_=wo_in[nt * HD:(nt + 1) * HD, :]), 16)

        @blk.scalar
        def _(scalar):
            for nt in range(NH):
                scalar.wait_ge(s_wop[nt % 2].h, (nt // 2 + 1) * 16)
                a = scalar.activation(out=wob[:, nt * D:(nt + 1) * D],
                                      in_=wostage[:, (nt % 2) * D:(nt % 2 + 1) * D],
                                      func=AF.Copy)
                s_wo.inc(a)

        atm_order = ([i for i in range(NH) if i % NHL < HHL] +
                     [i for i in range(NH) if i % NHL >= HHL])

        @blk.gpsimd
        def _(gpsimd):
            for j, i in enumerate(atm_order):
                gpsimd.wait_ge(s_ag1.h, 1 if i % NHL < HHL else 2)
                if j >= 1:
                    gpsimd.wait_ge(s_am.h, j * 16)
                src_t = agout1a if i % NHL < HHL else agout1b
                s_am.inc(gpsimd.indirect_dma_start(
                    out=atm[:, i * TS:(i + 1) * TS],
                    out_offset=None,
                    in_=src_t[:, :],
                    in_offset=IndirectOffsetOnAxis(ap=agidx[:, i:i + 1], axis=0)), 16)

        @blk.tensor
        def _(tensor):
            tensor.wait_ge(s_cst.h, CST_ALL)
            tensor.wait_ge(s_ev.h, EV_ALL)
            for ch2 in range(D // CH):
                m = tensor.matmul(op_ps[0][ch2][:], ones_row[:],
                                  wr_sb[:, ch2 * CH:(ch2 + 1) * CH],
                                  start=True, stop=True)
            s_bc.inc(m)

        @blk.vector
        def _(vector):
            vector.wait_ge(s_bc.h, 1)
            for ch2 in range(D // CH):
                vector.tensor_copy(out=wrb[:, ch2 * CH:(ch2 + 1) * CH],
                                   in_=op_ps[0][ch2][:])
            vfl(vector)
            s_xs.inc(vector.engine_nop())

    if stop_after == "p3a":
        p3_es.close()
        return finish(extra_waits=[(s_wo, NH), (s_am, NH * 16), (s_xs, 1)])

    SQ_P3 = s_sq.n
    TAB_P3 = s_tab.n
    ST_P3 = s_st.n
    for t in range(NTT):
        if True:
            blk = _blk_shim
            @blk.sync
            def _(sync, t=t):
                if t < 2:
                    sync.wait_ge(s_ev.h, EV_ALL)
                    sync.wait_ge(s_edp[t % 2].h, (ED0_ALL, ED1_ALL)[t % 2])
                else:
                    sync.wait_ge(s_x2.h, t - 1)
                s_xsp[t % 2].inc(sync.dma_start(
                    out=xs_t[:, (t % 2) * D:(t % 2 + 1) * D],
                    in_=x_sl[t * HD:(t + 1) * HD, :]), 16)

            @blk.tensor
            def _(tensor, t=t):
                tensor.wait_ge(s_wo.h, NH)
                if t == 0:
                    tensor.wait_ge(s_xs.h, 1)
                if t == 1:
                    tensor.wait_ge(s_ev.h, EV_ALL)
                if t >= 2:
                    tensor.wait_ge(s_x2.h, t - 1)
                for j, nt in enumerate(atm_order):
                    tensor.wait_ge(s_am.h, (j + 1) * 16)
                    for ch2 in range(D // CH):
                        m = tensor.matmul(
                            op_ps[t % 2][ch2][:],
                            atm[:, nt * TS + t * HD:nt * TS + (t + 1) * HD],
                            wob[:, nt * D + ch2 * CH:nt * D + (ch2 + 1) * CH],
                            start=(j == 0), stop=(j == NH - 1))
                s_op.inc(m)

            @blk.vector
            def _(vector, t=t):
                p = t % 4
                sl2 = (t % 2) * D
                vector.wait_ge(s_op.h, t + 1)
                vector.wait_ge(s_xsp[t % 2].h, (t // 2 + 1) * 16)
                for ch2 in range(D // CH):
                    vector.tensor_tensor(out=x2st[:, sl2 + ch2 * CH:sl2 + (ch2 + 1) * CH],
                                         in0=op_ps[t % 2][ch2][:],
                                         in1=xs_t[:, sl2 + ch2 * CH:sl2 + (ch2 + 1) * CH],
                                         op=ALU.add)
                vfl(vector)
                s_st.inc(vector.engine_nop())

            @blk.scalar
            def _(scalar, t=t):
                p = t % 4
                sl2 = (t % 2) * D
                x2v = x2st[:, sl2:sl2 + D]
                scalar.wait_ge(s_st.h, ST_P3 + t + 1)
                scalar.activation(out=junkb2[:], in_=x2v, func=AF.Identity,
                                  accum_out=small[:, 8 + p:9 + p])
                a = scalar.activation(out=junkb3[:], in_=x2v, func=AF.Square,
                                      accum_out=small[:, 12 + p:13 + p])
                s_fi.inc(a)
                scalar.wait_ge(s_fi.h, s_fi.n)
                a = scalar.activation(out=junkb3[:, 0:1], in_=small[:, 8 + p:9 + p],
                                      func=AF.Copy)
                s_fi.inc(a)

            @blk.vector
            def _(vector, t=t):
                p = t % 4
                vector.wait_ge(s_fi.h, 2 * (t + 1))
                vector.tensor_scalar(out=small[:, 16 + p:17 + p], in0=small[:, 8 + p:9 + p],
                                     scalar1=1.0 / D, scalar2=None, op0=ALU.mult)
                vector.tensor_scalar(out=small[:, 20 + p:21 + p], in0=small[:, 12 + p:13 + p],
                                     scalar1=1.0 / D, scalar2=EPS, op0=ALU.mult,
                                     op1=ALU.add)
                vfl(vector)
                vector.tensor_tensor(out=small[:, 28 + p:29 + p],
                                     in0=small[:, 16 + p:17 + p],
                                     in1=small[:, 16 + p:17 + p], op=ALU.mult)
                vfl(vector)
                vector.tensor_tensor(out=small[:, 20 + p:21 + p],
                                     in0=small[:, 20 + p:21 + p],
                                     in1=small[:, 28 + p:29 + p], op=ALU.subtract)
                s_w1.inc(vector.engine_nop())

            @blk.scalar
            def _(scalar, t=t):
                p = t % 4
                scalar.wait_ge(s_w1.h, t + 1)
                s_g.inc(scalar.activation(out=small[:, 28 + p:29 + p],
                                          in_=small[:, 20 + p:21 + p], func=AF.Sqrt))

            @blk.vector
            def _(vector, t=t):
                p = t % 4
                vector.wait_ge(s_g.h, t + 1)
                vector.reciprocal(out=small[:, 24 + p:25 + p], in_=small[:, 28 + p:29 + p])
                vfl(vector)
                vector.scalar_tensor_tensor(out=small[:, 0 + p:1 + p],
                                            in0=small[:, 16 + p:17 + p], scalar=-1.0,
                                            in1=small[:, 24 + p:25 + p],
                                            op0=ALU.mult, op1=ALU.mult)
                vector.tensor_scalar(out=small[:, 28 + p:29 + p],
                                     in0=small[:, 16 + p:17 + p],
                                     scalar1=-1.0, scalar2=None, op0=ALU.mult)
                vfl(vector)
                s_sq.inc(vector.engine_nop())

            @blk.scalar
            def _(scalar, t=t):
                p = t % 4
                sl2 = (t % 2) * D
                x2v = x2st[:, sl2:sl2 + D]
                scalar.wait_ge(s_sq.h, SQ_P3 + t + 1)
                if t >= 1:
                    scalar.wait_ge(s_x2.h, t)
                scalar.activation(out=x2lst[:, sl2:sl2 + D], in_=x2v, func=AF.Identity,
                                  bias=small[:, 0 + p:1 + p],
                                  scale=small[:, 24 + p:25 + p])
                a = scalar.activation(out=scr_f32[:], in_=x2v, func=AF.Identity,
                                      bias=small[:, 28 + p:29 + p], scale=1.0)
                s_tab.inc(a)

            @blk.vector
            def _(vector, t=t):
                p = t % 4
                vector.wait_ge(s_tab.h, TAB_P3 + t + 1)
                vector.scalar_tensor_tensor(out=junkb[:], in0=scr_f32[:], scalar=1.0,
                                            in1=wrb[:], op0=ALU.mult, op1=ALU.mult,
                                            accum_out=small[:, 4 + p:5 + p])
                vfl(vector)
                vector.tensor_tensor(out=sc_mine[:, t:t + 1], in0=small[:, 4 + p:5 + p],
                                     in1=small[:, 24 + p:25 + p], op=ALU.mult)
                vfl(vector)
                s_x2.inc(vector.engine_nop())

            @blk.sync
            def _(sync, t=t):
                sync.wait_ge(s_x2.h, t + 1)
                sync.wait_ge(s_tab.h, TAB_P3 + t + 1)
                sl2 = (t % 2) * D
                s_sd.inc(sync.dma_start(out=x2_hbm[t * HD:(t + 1) * HD, :],
                                        in_=x2st[:, sl2:sl2 + D]), 16)
                s_sd.inc(sync.dma_start(out=x2ln_hbm[t * HD:(t + 1) * HD, :],
                                        in_=x2lst[:, sl2:sl2 + D]), 16)

    if stop_after == "p3b":
        p3_es.close()
        return finish(extra_waits=[(s_x2, NTT), (s_sd, s_sd.n)])

    if True:
        blk = _blk_shim
        @blk.tensor
        def _(tensor):
            tensor.wait_ge(s_cst.h, CST_ALL)
            tensor.wait_ge(s_x2.h, NTT)
            m = tensor.matmul(op_ps[0][0][:, 0:1], ones_row[:], br_sb[:], start=True, stop=True)
            s_bc.inc(m)

        @blk.vector
        def _(vector):
            vector.wait_ge(s_bc.h, 2)
            vector.tensor_copy(out=small[:, 7:8], in_=op_ps[0][0][:, 0:1])
            vfl(vector)
            vector.tensor_scalar(out=sc_mine[:], in0=sc_mine[:],
                                 scalar1=small[:, 7:8], scalar2=None, op0=ALU.add)
            vfl(vector)
            s_x2.inc(vector.engine_nop())

        @blk.sync
        def _(sync):
            sync.wait_ge(s_x2.h, s_x2.n)
            for t in range(NTT):
                s_sd.inc(sync.dma_start(
                    out=agin2.rearrange("a (t p) -> (a t) p", p=HD)[t:t + 1, :],
                    in_=sc_mine[:, t:t + 1]), 16)

    SD_ALL = s_sd.n
    if True:
        blk = _blk_shim
        @blk.gpsimd
        def _(gpsimd):
            gpsimd.wait_ge(s_sd.h, SD_ALL)
            s_ag2.inc(gpsimd.collective_compute(
                "AllGather", ALU.bypass,
                replica_groups=[[0, 1, 2, 3, 4, 5, 6, 7]],
                ins=[agin2.ap().opt()], outs=[agout2.ap().opt()]), 1)

    p3_es.close()
    if stop_after == "proj":
        return finish(extra_waits=[(s_ag2, 1)])

    ffn_es = ExitStack()
    hT = SB(ffn_es, "hT", [HD, FSL * TS], BF16)
    sc_g = SB(ffn_es, "sc_g", [HD, S // HD], F32)

    f1_es = ExitStack()
    x2lt = SB(f1_es, "x2lt", [HD, DT_ * TS], BF16)
    w1b = [SB(f1_es, f"w1b{i}", [HD, DT_ * HD], BF16) for i in range(2)]
    w1stage = [SB(f1_es, f"w1stage{i}", [HD, DT_ * HD], F32) for i in range(2)]
    h_ps = [[PS(f1_es, f"h_ps{i}_{j}", [HD, CH]) for j in range(TS // CH)] for i in range(2)]
    thr_ps = PS(f1_es, "thr_ps", [HD, 1])
    cnt_ps = PS(f1_es, "cnt_ps", [1, 1])

    BC_B = s_bc.n
    CT_B = s_ct.n
    IT_B = s_it.n
    G_B = s_g.n
    QW_B = s_qw.n
    if True:
        blk = _blk_shim
        @blk.gpsimd
        def _(gpsimd):
            gpsimd.wait_ge(s_ag2.h, 1)
            s_sg.inc(gpsimd.indirect_dma_start(
                out=sc_g[:], out_offset=None, in_=agout2[:, :],
                in_offset=IndirectOffsetOnAxis(ap=scidx[:, 0:1], axis=0)), 16)

        @blk.sync
        def _(sync):
            sync.wait_ge(s_sd.h, SD_ALL)
            for dt_ in range(DT_):
                s_xt2.inc(sync.dma_start(
                    out=x2lt[:, dt_ * TS:(dt_ + 1) * TS],
                    in_=x2ln_hbm[:, dt_ * HD:(dt_ + 1) * HD],
                    transpose=True), 16)

        @blk.vector
        def _(vector):
            vector.wait_ge(s_sg.h, 16)
            vector.wait_ge(s_cst.h, CST_ALL)
            vector.tensor_scalar(out=tiny[:, 4:5], in0=ones_row[0:1, 0:1],
                                 scalar1=-1e4, scalar2=None, op0=ALU.mult)
            vector.tensor_scalar(out=tiny[:, 5:6], in0=ones_row[0:1, 0:1],
                                 scalar1=1e4, scalar2=None, op0=ALU.mult)
            vfl(vector)
            s_it.inc(vector.engine_nop())
            for it in range(N_ITER):
                vector.tensor_tensor(out=tiny[:, 6:7], in0=tiny[:, 4:5],
                                     in1=tiny[:, 5:6], op=ALU.add)
                vfl(vector)
                vector.tensor_scalar(out=tiny[:, 6:7], in0=tiny[:, 6:7], scalar1=0.5,
                                     scalar2=None, op0=ALU.mult)
                vfl(vector)
                s_it.inc(vector.engine_nop())
                vector.wait_ge(s_bc.h, BC_B + it + 1)
                vector.tensor_scalar(out=junkb[:, 0:S // HD], in0=sc_g[:],
                                     scalar1=thr_ps[:, 0:1], scalar2=0.0,
                                     op0=ALU.is_gt, op1=ALU.add,
                                     accum_out=small[:, 30:31])
                vfl(vector)
                s_it.inc(vector.engine_nop())
                vector.wait_ge(s_ct.h, CT_B + it + 1)
                vector.tensor_single_scalar(out=tiny_i[:, 0:1], in_=cnt_ps[:],
                                            scalar=float(KSEL), op=ALU.is_ge)
                vector.tensor_single_scalar(out=tiny_i[:, 1:2], in_=cnt_ps[:],
                                            scalar=float(KSEL), op=ALU.is_lt)
                vfl(vector)
                vector.copy_predicated(out=tiny[:, 4:5], mask=tiny_i[:, 0:1],
                                       data=tiny[:, 6:7])
                vector.copy_predicated(out=tiny[:, 5:6], mask=tiny_i[:, 1:2],
                                       data=tiny[:, 6:7])
                vfl(vector)
            s_it.inc(vector.engine_nop())
            vector.wait_ge(s_bc.h, BC_B + N_ITER + 1)
            vector.tensor_scalar(out=mask_sb[:], in0=sc_mine[:],
                                 scalar1=thr_ps[:, 0:1], scalar2=0.0,
                                 op0=ALU.is_gt, op1=ALU.add)
            vfl(vector)
            s_ms.inc(vector.engine_nop())

    if "scg" in tap_t or "thr" in tap_t or "sc" in tap_t:
        if True:
            blk = _blk_shim
            @blk.sync
            def _(sync):
                sync.wait_ge(s_ms.h, 1)
                if "scg" in tap_t:
                    s_out.inc(sync.dma_start(out=tap_t["scg"][:], in_=sc_g[:]), 16)
                if "thr" in tap_t:
                    s_out.inc(sync.dma_start(out=tap_t["thr"][:], in_=tiny[:, 4:8]), 16)
                if "sc" in tap_t:
                    s_out.inc(sync.dma_start(out=tap_t["sc"][:], in_=agin2[:]), 16)
    if "x2" in tap_t or "x2ln" in tap_t:
        if True:
            blk = _blk_shim
            @blk.sync
            def _(sync):
                sync.wait_ge(s_sd.h, SD_ALL)
                if "x2" in tap_t:
                    s_out.inc(sync.dma_start(out=tap_t["x2"][:], in_=x2_hbm[:]), 16)
                if "x2ln" in tap_t:
                    s_out.inc(sync.dma_start(out=tap_t["x2ln"][:], in_=x2ln_hbm[:]), 16)

    if True:
        blk = _blk_shim
        @blk.sync
        def _(sync):
            for f in range(FSL):
                if f < 2:
                    sync.wait_ge(s_op.h, NTT)
                else:
                    sync.wait_ge(s_h.h, f - 1)
                s_w1p[f % 2].inc(sync.dma_start(
                    out=w1stage[f % 2][:],
                    in_=w1_in[f * HD:(f + 1) * HD, :]), 16)

        @blk.tensor
        def _(tensor):
            tensor.wait_ge(s_cst.h, CST_ALL)
            tensor.wait_ge(s_xt2.h, DT_ * 16)
            S_OFF = 8 if FSL >= N_ITER + 8 else 0
            for f in range(FSL):
                if S_OFF <= f < N_ITER + S_OFF:
                    it = f - S_OFF
                    tensor.wait_ge(s_it.h, IT_B + 2 + 2 * it)
                    s_bc.inc(tensor.matmul(thr_ps[:], ones_row[:], tiny[:, 6:7],
                                            start=True, stop=True))
                    tensor.wait_ge(s_it.h, IT_B + 3 + 2 * it)
                    s_ct.inc(tensor.matmul(cnt_ps[:], small[:, 30:31], ones_colf[:],
                                           start=True, stop=True))
                elif f == N_ITER + S_OFF:
                    tensor.wait_ge(s_it.h, IT_B + 2 + 2 * N_ITER)
                    s_bc.inc(tensor.matmul(thr_ps[:], ones_row[:], tiny[:, 4:5],
                                            start=True, stop=True))
                tensor.wait_ge(s_qw.h, QW_B + f + 1)
                if f >= 2:
                    tensor.wait_ge(s_g.h, G_B + f - 1)
                for ch2 in range(TS // CH):
                    for dt_ in range(DT_):
                        m = tensor.matmul(
                            h_ps[f % 2][ch2][:],
                            w1b[f % 2][:, dt_ * HD:(dt_ + 1) * HD],
                            x2lt[:, dt_ * TS + ch2 * CH:dt_ * TS + (ch2 + 1) * CH],
                            start=(dt_ == 0), stop=(dt_ == DT_ - 1))
                s_h.inc(m)
            for it in range(max(0, min(FSL - S_OFF, N_ITER)), N_ITER):
                tensor.wait_ge(s_it.h, IT_B + 2 + 2 * it)
                s_bc.inc(tensor.matmul(thr_ps[:], ones_row[:], tiny[:, 6:7],
                                       start=True, stop=True))
                tensor.wait_ge(s_it.h, IT_B + 3 + 2 * it)
                s_ct.inc(tensor.matmul(cnt_ps[:], small[:, 30:31], ones_colf[:],
                                       start=True, stop=True))
            if FSL <= N_ITER + S_OFF:
                tensor.wait_ge(s_it.h, IT_B + 2 + 2 * N_ITER)
                s_bc.inc(tensor.matmul(thr_ps[:], ones_row[:], tiny[:, 4:5],
                                       start=True, stop=True))

        @blk.scalar
        def _(scalar):
            def gelu(f):
                scalar.wait_ge(s_h.h, f + 1)
                for ch2 in range(TS // CH):
                    a = scalar.activation(
                        out=hT[:, f * TS + ch2 * CH:f * TS + (ch2 + 1) * CH],
                        in_=h_ps[f % 2][ch2][:],
                        func=AF.Gelu_apprx_tanh, bias=b1_sb[:, f:f + 1], scale=1.0)
                s_g.inc(a)
            for f in range(FSL):
                scalar.wait_ge(s_w1p[f % 2].h, (f // 2 + 1) * 16)
                if f >= 2:
                    scalar.wait_ge(s_h.h, f - 1)
                a = scalar.activation(out=w1b[f % 2][:], in_=w1stage[f % 2][:],
                                      func=AF.Copy)
                s_qw.inc(a)
                if f >= 1:
                    gelu(f - 1)
            gelu(FSL - 1)

    H_ALL = s_h.n
    G_ALL = s_g.n
    if stop_after == "w1":
        f1_es.close()
        ffn_es.close()
        return finish(extra_waits=[(s_g, G_ALL)])
    if "hT" in tap_t:
        if True:
            blk = _blk_shim
            @blk.sync
            def _(sync):
                sync.wait_ge(s_g.h, G_ALL)
                s_out.inc(sync.dma_start(out=tap_t["hT"][:], in_=hT[:, 0:min(16, FSL) * TS]), 16)

    f1_es.close()

    f2_es = ExitStack()
    w2b = [SB(f2_es, f"w2b{i}", [HD, FSL * HD], BF16) for i in range(2)]
    w2stage = [SB(f2_es, f"w2stage{i}", [HD, FSL * HD // max(1, FSL // 16)], F32)
               for i in range(2)]
    o2fb = [SB(f2_es, f"o2fb{i}", [HD, TS], BF16) for i in range(2)]
    o2_ps = [[PS(f2_es, f"o2_ps{i}_{j}", [HD, CH]) for j in range(TS // CH)] for i in range(2)]

    CT_W2 = s_ct.n

    if True:
        blk = _blk_shim
        @blk.scalar
        def _(scalar):
            nsub = max(1, FSL // 16)
            fsub = FSL // nsub
            nld = DSL * nsub
            scalar.wait_ge(s_ct.h, CT_W2)

            def ld(k):
                ds_, sb2 = divmod(k, nsub)
                s_w2p[k % 2].inc(scalar.dma_start(
                    out=w2stage[k % 2][:],
                    in_=w2_in[ds_ * HD:(ds_ + 1) * HD,
                              sb2 * fsub * HD:(sb2 + 1) * fsub * HD]), 16)
            scalar.wait_ge(s_h.h, FSL)
            ld(0); ld(1)
            for ds in range(DSL):
                for sb_ in range(nsub):
                    k = ds * nsub + sb_
                    scalar.wait_ge(s_w2p[k % 2].h, (k // 2 + 1) * 16)
                    if ds >= 2 and sb_ == 0:
                        scalar.wait_ge(s_o2.h, ds - 1)
                    a = scalar.activation(
                        out=w2b[ds % 2][:, sb_ * fsub * HD:(sb_ + 1) * fsub * HD],
                        in_=w2stage[k % 2][:], func=AF.Copy)
                    s_ct.inc(a)
                    if k + 2 < nld:
                        scalar.wait_ge(s_ct.h, CT_W2 + k + 1)
                        ld(k + 2)

        @blk.tensor
        def _(tensor):
            tensor.wait_ge(s_g.h, G_ALL)
            for ds in range(DSL):
                tensor.wait_ge(s_ct.h, CT_W2 + (ds + 1) * max(1, FSL // 16))
                if ds >= 2:
                    tensor.wait_ge(s_te.h, ds - 1)
                for ft in range(FSL):
                    for ch2 in range(TS // CH):
                        m = tensor.matmul(
                            o2_ps[ds % 2][ch2][:],
                            w2b[ds % 2][:, ft * HD:(ft + 1) * HD],
                            hT[:, ft * TS + ch2 * CH:ft * TS + (ch2 + 1) * CH],
                            start=(ft == 0), stop=(ft == FSL - 1))
                s_o2.inc(m)

        @blk.vector
        def _(vector):
            for ds in range(DSL):
                vector.wait_ge(s_o2.h, ds + 1)
                if ds >= 2:
                    vector.wait_ge(s_owp[ds % 2].h, (ds // 2) * 16)
                for ch2 in range(TS // CH):
                    vector.tensor_scalar(out=o2fb[ds % 2][:, ch2 * CH:(ch2 + 1) * CH],
                                         in0=o2_ps[ds % 2][ch2][:],
                                         scalar1=b2c_sb[:, ds:ds + 1], scalar2=None,
                                         op0=ALU.add)
                vfl(vector)
                s_te.inc(vector.engine_nop())

        @blk.sync
        def _(sync):
            for ds in range(DSL):
                sync.wait_ge(s_te.h, ds + 1)
                s_owp[ds % 2].inc(sync.dma_start(
                    out=o2_hbm[ds * HD:(ds + 1) * HD, :],
                    in_=o2fb[ds % 2][:]), 16)

    f2_es.close()
    if stop_after == "w2":
        ffn_es.close()
        return finish(extra_waits=[(s_owp[0], s_owp[0].n), (s_owp[1], s_owp[1].n)])
    if "o2" in tap_t:
        if True:
            blk = _blk_shim
            @blk.sync
            def _(sync):
                sync.wait_ge(s_owp[0].h, s_owp[0].n)
                sync.wait_ge(s_owp[1].h, s_owp[1].n)
                s_out.inc(sync.dma_start(out=tap_t["o2"][:], in_=o2_hbm[:]), 16)

    ep_es = ExitStack()
    oin = SB(ep_es, "oin", [HD, 2 * D], BF16)
    x2in = SB(ep_es, "x2in", [HD, 2 * D], F32)
    ostage = SB(ep_es, "ostage", [HD, 4 * D], F32)
    if True:
        blk = _blk_shim
        @blk.sync
        def _(sync):
            sync.wait_ge(s_owp[0].h, s_owp[0].n)
            sync.wait_ge(s_owp[1].h, s_owp[1].n)
            def wr(t):
                sl4 = (t % 4) * D
                sync.wait_ge(s_fe.h, t + 1)
                s_oup[t % 2].inc(sync.dma_start(out=out_sl[t * HD:(t + 1) * HD, :],
                                                in_=ostage[:, sl4:sl4 + D]), 16)
            for t in range(NTT):
                sl2 = (t % 2) * D
                if t >= 2:
                    sync.wait_ge(s_fe.h, t - 1)
                s_fip[t % 2].inc(sync.dma_start(out=oin[:, sl2:sl2 + D],
                                                in_=o2_hbm[:, t * HD:(t + 1) * HD],
                                                transpose=True), 16)
                if t >= 2:
                    wr(t - 2)
            wr(NTT - 2)
            wr(NTT - 1)

        @blk.scalar
        def _(scalar):
            scalar.wait_ge(s_ms.h, 1)
            scalar.wait_ge(s_owp[0].h, s_owp[0].n)
            scalar.wait_ge(s_owp[1].h, s_owp[1].n)
            for t in range(NTT):
                sl2 = (t % 2) * D
                if t >= 2:
                    scalar.wait_ge(s_fe.h, t - 1)
                s_fip[t % 2].inc(scalar.dma_start(out=x2in[:, sl2:sl2 + D],
                                                  in_=x2_hbm[t * HD:(t + 1) * HD, :]), 16)

        @blk.vector
        def _(vector):
            vector.wait_ge(s_ms.h, 1)
            for t in range(NTT):
                sl2 = (t % 2) * D
                sl4 = (t % 4) * D
                vector.wait_ge(s_fip[t % 2].h, (t // 2 + 1) * 32)
                if t >= 4:
                    vector.wait_ge(s_oup[t % 2].h, ((t - 4) // 2 + 1) * 16)
                m = vector.scalar_tensor_tensor(out=ostage[:, sl4:sl4 + D],
                                                in0=oin[:, sl2:sl2 + D],
                                                scalar=mask_sb[:, t:t + 1],
                                                in1=x2in[:, sl2:sl2 + D],
                                                op0=ALU.mult, op1=ALU.add)
                s_fe.inc(m)

    ep_es.close()
    ffn_es.close()
    return finish()


_NC_CACHE = {}


def _get_nc(stop_after=None, taps=()):
    key = (stop_after, tuple(sorted(taps)))
    if key not in _NC_CACHE:
        _NC_CACHE[key] = build_nc(stop_after, taps)
    return _NC_CACHE[key]


def _split_hi_lo(r64):
    hi = np.clip(r64.astype(np.float32), -PCLIP, PCLIP)
    lo = (r64 - hi.astype(np.float64)).astype(np.float32)
    return hi, lo


def make_angle_tables():
    i = np.arange(HD // 2, dtype=np.float64)
    fraction = (2.0 * i) / HD
    timescale = 10000.0 ** fraction
    pos = np.arange(S, dtype=np.float64)
    ang = pos[None, :] / timescale[:, None]
    ang2 = np.repeat(ang, 2, axis=0)
    sign = np.where(np.arange(HD) % 2 == 0, -1.0, 1.0)[:, None]
    a_sin = sign * ang2
    a_cos = np.pi / 2 - ang2
    fold = lambda a: np.mod(a + np.pi, 2 * np.pi) - np.pi
    sh, sl = _split_hi_lo(fold(a_sin))
    ch_, cl = _split_hi_lo(fold(a_cos))
    return sh, sl, ch_, cl


def make_consts():
    tri = (np.arange(HD)[:, None] <= np.arange(HD)[None, :])
    import ml_dtypes
    tri = tri.astype(ml_dtypes.bfloat16)
    ident = np.eye(HD, dtype=np.float32)
    ones = np.ones((HD, 1), dtype=np.float32)
    return tri, ident, ones


def kernel(L0_attn_x, L0_attn_wq, L0_attn_wk, L0_attn_wv, L0_attn_wo,
           L0_ffn_w_r, L0_ffn_b_r, L0_ffn_w1, L0_ffn_b1, L0_ffn_w2, L0_ffn_b2,
           stop_after=None, taps=()):
    x = np.ascontiguousarray(np.asarray(L0_attn_x, dtype=np.float32))
    wq = np.asarray(L0_attn_wq, dtype=np.float32).reshape(D, NH, HD)
    wk = np.asarray(L0_attn_wk, dtype=np.float32).reshape(D, NH, HD)
    wv = np.asarray(L0_attn_wv, dtype=np.float32).reshape(D, NH, HD)
    wo = np.ascontiguousarray(np.asarray(L0_attn_wo, dtype=np.float32).reshape(NH * HD, D))
    w_r = np.asarray(L0_ffn_w_r, dtype=np.float32).reshape(1, D)
    b_r = np.asarray(L0_ffn_b_r, dtype=np.float32).reshape(1, 1)
    w1f = np.asarray(L0_ffn_w1, dtype=np.float32)
    w1 = np.ascontiguousarray(
        w1f.reshape(D // HD, HD, DF // HD, HD).transpose(2, 1, 0, 3)
        .reshape(DF, D))
    b1c = np.ascontiguousarray(
        np.asarray(L0_ffn_b1, dtype=np.float32).reshape(DF // HD, HD).T)
    w2f = np.asarray(L0_ffn_w2, dtype=np.float32)
    w2 = np.ascontiguousarray(
        w2f.reshape(DF // HD, HD, D // HD, HD).transpose(2, 1, 0, 3)
        .reshape(D, DF))
    b2c = np.ascontiguousarray(
        np.asarray(L0_ffn_b2, dtype=np.float32).reshape(D // HD, HD).T)
    sh, sl, ch_, cl = make_angle_tables()
    tri, ident, ones = make_consts()

    p = np.arange(HD, dtype=np.uint64)
    in_maps = []
    for c in range(8):
        row, ts = c // 4, c % 4
        hg = c % 4
        agidx = np.empty((HD, NH), dtype=np.uint32)
        hhl = max(1, NHL // 2)
        for i in range(NH):
            r = row * 4 + i // NHL
            l = i % NHL
            lh = l - hhl if l >= hhl else l
            nh_half = hhl if l < hhl else NHL - hhl
            r_glob = r * (nh_half * HD) + lh * HD + p
            agidx[:, i] = (r_glob * (S // TS) + ts).astype(np.uint32)
        scidx = (row * HD + p).astype(np.uint32).reshape(HD, 1)
        in_maps.append({
            "x_row": x[row],
            "x_sl": np.ascontiguousarray(x[row, ts * TS:(ts + 1) * TS]),
            "wq_g": np.ascontiguousarray(wq[:, NHL * hg:NHL * hg + NHL, :].reshape(D, NHL * HD)),
            "wk_g": np.ascontiguousarray(wk[:, NHL * hg:NHL * hg + NHL, :].reshape(D, NHL * HD)),
            "wv_g": np.ascontiguousarray(wv[:, NHL * hg:NHL * hg + NHL, :].reshape(D, NHL * HD)),
            "wo_in": wo, "wr_in": w_r, "br_in": b_r,
            "w1_in": w1, "b1c_in": b1c, "w2_in": w2, "b2c_in": b2c,
            "ash_in": sh, "asl_in": sl, "ach_in": ch_, "acl_in": cl,
            "tri_in": tri, "id_in": ident, "ones_in": ones,
            "onesr_in": np.ones((1, HD), dtype=np.float32),
            "agidx_in": agidx, "scidx_in": scidx,
        })
    nc = _get_nc(stop_after, taps)
    res = run_bass_kernel_spmd(nc, in_maps, core_ids=list(range(8)))
    if stop_after is not None or taps:
        return res
    out = np.empty((B, S, D), dtype=np.float32)
    for c in range(8):
        row, ts = c // 4, c % 4
        out[row, ts * TS:(ts + 1) * TS] = np.asarray(res.results[c]["out_sl"])
    return out
